# version 31
# baseline (speedup 1.0000x reference)
import numpy as np
import ml_dtypes
from contextlib import ExitStack

import concourse.bass as bass
import concourse.mybir as mybir
from concourse.bass_utils import run_bass_kernel_spmd

F32 = mybir.dt.float32
BF16 = mybir.dt.bfloat16
AF = mybir.ActivationFunctionType
ALU = mybir.AluOpType
AXX = mybir.AxisListType.X

D = 1024
DC = 8
H = 4096
HC = 32
HEADS = 8
DH = 128
FEAT = 64
NTOK = 4096
TILES = [456] * 8 + [448]
NT = len(TILES)
MAXE = 462
XE = 464
VST = 130
assert sum(TILES) == NTOK
EPS_LN = 1e-5
EPS_Z = 1e-8

_CACHED = {}


class SemC:
    def __init__(self, name):
        self.name = name
        self.h = None
        self.n = 0

    def inc(self, amt=1):
        self.n += amt
        return self.n


class Rec:
    def __init__(self, eng_sems=None):
        self.ops = []
        self.eng_sems = eng_sems or {}

    def op(self, eng, fn=None, waits=(), inc=None):
        incs = ()
        if inc is not None:
            incs = tuple(inc) if isinstance(inc, list) else (inc,)
        waits = list(waits)
        own = self.eng_sems.get(eng)
        if own is not None and fn is not None:
            own_amt = sum(a for (s, a) in incs if s is own)
            if own_amt == 0:
                own.inc(1)
                incs = incs + ((own, 1),)
                own_amt = 1
            waits.append((own, own.n - own_amt))
        waits = tuple((s, v) for (s, v) in waits if v > 0)
        for s, v in waits:
            assert v <= s.n, f"wait-on-future: eng={eng} sem={s.name} v={v} planned={s.n}"
        self.ops.append((eng, fn, waits, incs))


def cinc(sem):
    sem.inc(1)
    return (sem, 1)


def build_nc():
    nc = bass.Bass("TRN2", target_bir_lowering=False, debug=False)

    xT = nc.dram_tensor("xT", [D, NTOK + 6], F32, kind="ExternalInput").ap()
    wqkv = nc.dram_tensor("wqkv", [D, 3 * D], F32, kind="ExternalInput").ap()
    wout = nc.dram_tensor("wout", [D, D], F32, kind="ExternalInput").ap()
    proj = nc.dram_tensor("proj", [DH, FEAT], F32, kind="ExternalInput").ap()
    w1t = nc.dram_tensor("w1t", [D, H], F32, kind="ExternalInput").ap()
    w1n = nc.dram_tensor("w1n", [H, D], F32, kind="ExternalInput").ap()
    w2t = nc.dram_tensor("w2t", [H, D], F32, kind="ExternalInput").ap()
    w2n = nc.dram_tensor("w2n", [D, H], F32, kind="ExternalInput").ap()
    b1c = nc.dram_tensor("b1c", [128, HC], F32, kind="ExternalInput").ap()
    b2c = nc.dram_tensor("b2c", [128, DC], F32, kind="ExternalInput").ap()
    boutc = nc.dram_tensor("boutc", [128, DC], F32, kind="ExternalInput").ap()
    maskA3 = nc.dram_tensor("maskA3", [128, 3 * 128], F32, kind="ExternalInput").ap()
    maskB = nc.dram_tensor("maskB", [6, 128], F32, kind="ExternalInput").ap()
    identf = nc.dram_tensor("identf", [128, 128], F32, kind="ExternalInput").ap()
    onesb = nc.dram_tensor("onesb", [128, 1], BF16, kind="ExternalInput").ap()
    ones1 = nc.dram_tensor("ones1", [1, 128], F32, kind="ExternalInput").ap()
    cconst = nc.dram_tensor("cconst", [128, 3], F32, kind="ExternalInput").ap()
    yT = nc.dram_tensor("yT", [D, NTOK], F32, kind="ExternalOutput").ap()

    scw1 = nc.dram_tensor("scw1", [8 * 128 * DC, 512], BF16, kind="Internal").ap()
    scw2 = nc.dram_tensor("scw2", [DC * 128 * HC, 128], BF16, kind="Internal").ap()
    scqkv = nc.dram_tensor("scqkv", [24 * 128 * DC, 128], BF16, kind="Internal").ap()

    s_pe = SemC("s_pe")
    s_act = SemC("s_act")
    s_dve = SemC("s_dve")
    s_gpe = SemC("s_gpe")
    s_const = SemC("s_const")
    s_wload = [SemC(f"s_wl{i}") for i in range(4)]
    s_scwr = [SemC(f"s_scwr{i}") for i in range(4)]
    s_al = [SemC(f"s_al{i}") for i in range(4)]
    s_scq = [SemC(f"s_scq{i}") for i in range(4)]
    s_fw1 = [SemC("s_fw1a"), SemC("s_fw1b")]
    s_fw2 = [SemC("s_fw2a"), SemC("s_fw2b")]
    s_fqk = [SemC("s_fqka"), SemC("s_fqkb")]
    s_fvr = [SemC("s_fvra"), SemC("s_fvrb")]
    s_x = [SemC("s_x0"), SemC("s_x1")]
    s_out = SemC("s_out")
    sems = ([s_pe, s_act, s_dve, s_gpe, s_const, s_out]
            + s_wload + s_scwr + s_al + s_scq
            + s_fw1 + s_fw2 + s_fqk + s_fvr + s_x)
    rec = Rec(eng_sems={"scalar": s_act, "vector": s_dve, "gpsimd": s_gpe})

    with ExitStack() as ctx:
        sb = lambda n, s, d: ctx.enter_context(nc.sbuf_tensor(n, s, d))
        psa = lambda n, s, d: ctx.enter_context(nc.psum_tensor(n, s, d))

        xb = [sb("xb0", [128, DC * XE], F32), sb("xb1", [128, DC * XE], F32)]
        xn_a = sb("xn_a", [128, DC * MAXE], BF16)
        xn_b = sb("xn_b", [128, DC * MAXE], BF16)
        xc_a = sb("xc_a", [128, 2 * MAXE], BF16)
        sq_a = sb("sq_a", [128, 2 * MAXE], BF16)
        xc_b = sb("xc_b", [128, 2 * MAXE], BF16)
        sq_b = sb("sq_b", [128, 2 * MAXE], BF16)
        scrA = [sb(f"scrA{i}", [128, MAXE], F32) for i in range(3)]
        scrB = [sb(f"scrB{i}", [128, MAXE], F32) for i in range(3)]
        rows_t = sb("rows_t", [1, 4 * MAXE], F32)
        g = sb("g", [128, HC * MAXE], BF16)
        qkrot = sb("qkrot", [128, 2 * MAXE], BF16)
        qphi2 = [sb("qphi0", [128, 4 * MAXE], BF16), sb("qphi1", [128, 4 * MAXE], BF16)]
        kphi2 = [sb("kphi0", [128, 4 * MAXE], BF16), sb("kphi1", [128, 4 * MAXE], BF16)]
        vtok2 = [sb("vtok0", [128, HEADS * 4 * VST], BF16),
                 sb("vtok1", [128, HEADS * 4 * VST], BF16)]
        smA = sb("smA", [128, 2 * 128], BF16)
        smB = sb("smB", [6, 2 * 128], BF16)
        attn_n = sb("attn_n", [128, 2 * 128], F32)
        attnT = sb("attnT", [128, HEADS * MAXE], BF16)
        zcol = sb("zcol", [128, 4], F32)
        w1s = sb("w1s", [128, 2 * 4096], BF16)
        w2s = sb("w2s", [128, 2 * 4096], BF16)
        qks = sb("qks", [128, 2 * 1024], BF16)
        vrs = sb("vrs", [128, 2 * 512], BF16)
        woutb = sb("woutb", [128, DC * 1024], BF16)
        projb = sb("projb", [128, FEAT], BF16)
        alpha1 = sb("alpha1", [128, HC], F32)
        alpha2 = sb("alpha2", [128, DC], F32)
        b1s = sb("b1s", [128, HC], F32)
        b2s = sb("b2s", [128, DC], F32)
        bouts = sb("bouts", [128, DC], F32)
        mA = sb("mA", [128, 3 * 128], F32)
        mB = sb("mB", [6, 128], F32)
        id32 = sb("id32", [128, 128], F32)
        onb = sb("onb", [128, 1], BF16)
        on1 = sb("on1", [1, 128], F32)
        ccs = sb("ccs", [128, 3], F32)
        pf32 = sb("pf32", [128, 2 * 1024], F32)
        patm = sb("patm", [128, 4], F32)

        def overlay(name, shape, dtype, host):
            hnd = nc.virtual_sbuf_tensor(name, shape, dtype)
            nc._tensor(hnd.name, hnd.shape, hnd.dtype, type="SB")
            mloc = nc.lookup_mloc(hnd)
            mloc.allocated = True
            mloc.addr = nc.lookup_mloc(host).addr
            return hnd

        pa = overlay("pa", [128, 2 * 1024], F32, vtok2[1])
        pbf = overlay("pbf", [128, 3 * 1024], BF16, attnT)
        pf32c = overlay("pf32c", [128, 1024], F32, xn_b)

        SLW = 1024
        PB = [psa(f"PB{i}", [128, 512], F32) for i in range(8)]

        r1a = rows_t[0:1, 0:MAXE]
        r2a = rows_t[0:1, MAXE:2 * MAXE]
        r1b = rows_t[0:1, 2 * MAXE:3 * MAXE]
        r2b = rows_t[0:1, 3 * MAXE:4 * MAXE]

        for s in sems:
            s.h = ctx.enter_context(nc.semaphore(s.name))

        bank_free = {}

        def bank_wait(bid):
            w = bank_free.get(bid)
            return list(w) if w else []

        def t_mm(out, lhsT, rhs, start, stop, waits=(), inc=None):
            rec.op("tensor",
                   lambda E, o=out, l=lhsT, r=rhs, st=start, sp=stop:
                   E.matmul(o, l, r, start=st, stop=sp),
                   waits=waits, inc=inc)

        def gdma(sem, dst, src, waits=()):
            v = sem.inc(16)
            rec.op("gpsimdq", lambda E, d=dst, s=src: E.dma_start(out=d, in_=s),
                   waits=waits, inc=(sem, 16))
            return v

        def sdma(sem, dst, src, waits=()):
            v = sem.inc(16)
            rec.op("sync", lambda E, d=dst, s=src: E.dma_start(out=d, in_=s),
                   waits=waits, inc=(sem, 16))
            return v

        S1c = {"xn": xn_a, "xc": xc_a, "sq": sq_a, "scr": scrA,
               "r1": r1a, "r2": r2a, "sbk": (0, 1),
               "xc_free": [None, None], "sq_free": [0, 0], "xn_rd": 0}
        S2c = {"xn": xn_b, "xc": xc_b, "sq": sq_b, "scr": scrB,
               "r1": r1b, "r2": r2b, "sbk": (4, 5),
               "xc_free": [None, None], "sq_free": [0, 0], "xn_rd": 0}

        offs = [sum(TILES[:i]) for i in range(NT)]
        xload_v = {}
        out_v = {}

        def load_x(i):
            buf = xb[i % 2]
            E = TILES[i] + 6
            waits = [] if i < 2 else [(s_out, out_v[i - 2])]
            sem = s_x[i % 2]
            for c in range(DC):
                gdma(sem, buf[:, c * XE: c * XE + E],
                     xT[c * 128:(c + 1) * 128, offs[i]: offs[i] + E],
                     waits=waits if c == 0 else ())
            xload_v[i] = (sem, sem.n)

        load_x(0)
        load_x(1)

        gdma(s_const, mA[:], maskA3)
        gdma(s_const, mB[:], maskB)
        gdma(s_const, id32[:], identf)
        gdma(s_const, onb[:], onesb)
        gdma(s_const, on1[:], ones1)
        gdma(s_const, ccs[:], cconst)
        gdma(s_const, b1s[:], b1c)
        gdma(s_const, b2s[:], b2c)
        gdma(s_const, bouts[:], boutc)
        gdma(s_const, pf32[:, 0:FEAT], proj)
        consts_v = s_const.n
        rec.op("gpsimd", lambda E: E.tensor_copy(projb[:], pf32[:, 0:FEAT]),
               waits=[(s_const, consts_v)], inc=cinc(s_gpe))
        projb_v = s_gpe.n
        projb_gate = (s_gpe, projb_v)
        rec.op("gpsimd", lambda E: E.tensor_scalar_mul(b2s[:], b2s[:], 0.5),
               waits=[(s_const, consts_v)], inc=cinc(s_gpe))
        for h in range(HEADS):
            for tc in range(4):
                col = (h * 4 + tc) * VST + 128
                rec.op("gpsimd", lambda E, c=col: E.memset(vtok2[0][:, c:c + 1], 1.0),
                       inc=cinc(s_gpe))
        v_ones = {0: s_gpe.n, 1: 0}

        scw1_g = scw1.rearrange("(g p k) c -> g p k c", p=128, k=DC)
        scw2_m = scw2.rearrange("(m p k) c -> m p k c", p=128, k=HC)
        scq_o = scqkv.rearrange("(o p k) c -> o p k c", p=128, k=DC)

        def pfs(sl):
            return pf32[:, sl * SLW:(sl + 1) * SLW] if sl < 2 else pf32c[:, 0:SLW]

        def pbs(sl):
            return pbf[:, sl * SLW:(sl + 1) * SLW]

        NSL = 3
        slot_gate = [projb_gate, None, None]
        wrf = [0] * NSL

        def conv_step(it, load_src, consume_fn):
            sl = it % NSL
            w = [slot_gate[sl]] if slot_gate[sl] else []
            v_l = sdma(s_wload[sl], pfs(sl), load_src, waits=w)
            slot_gate[sl] = consume_fn(sl, [(s_wload[sl], v_l)])

        def sign_and_write(write_dst_fn):
            def consume(sl, waits):
                if wrf[sl]:
                    waits = waits + [(s_scwr[sl], wrf[sl])]
                rec.op("scalar",
                       lambda E, sl=sl: E.sign(pbs(sl), pfs(sl)),
                       waits=waits, inc=cinc(s_act))
                sv = s_act.n
                rec.op("scalarq", write_dst_fn(sl), waits=[(s_act, sv)],
                       inc=(s_scwr[sl], 16))
                wrf[sl] = s_scwr[sl].inc(16)
                return (s_act, sv)
            return consume

        w1grp_wait = {}

        a1col_v = [0] * HC
        a2col_v = [0] * DC
        deferred = {}
        ASL = 2
        aslot_gate = [None] * ASL
        alpha_loads = {}

        def alpha_issue(it, load_src):
            sl = it % ASL
            w = [aslot_gate[sl]] if aslot_gate[sl] else []
            alpha_loads[it] = gdma(s_al[sl], pa[:, sl * SLW:(sl + 1) * SLW],
                                   load_src, waits=w)

        def emit_alpha1():
            srcs = [w1n[hcn * 128:(hcn + 1) * 128, :] for hcn in range(HC)]
            for it in range(ASL):
                alpha_issue(it, srcs[it])
            for hcn in range(HC):
                sl = hcn % ASL
                rec.op("vector",
                       lambda E, sl=sl, c=hcn: E.tensor_reduce(
                           alpha1[:, c:c + 1], pa[:, sl * SLW:(sl + 1) * SLW],
                           axis=AXX, op=ALU.add, apply_absolute_value=True),
                       waits=[(s_al[sl], alpha_loads[hcn])], inc=cinc(s_dve))
                rec.op("vector",
                       lambda E, c=hcn: E.tensor_scalar_mul(
                           alpha1[:, c:c + 1], alpha1[:, c:c + 1], 1.0 / D),
                       inc=cinc(s_dve))
                a1col_v[hcn] = s_dve.n
                aslot_gate[sl] = (s_dve, s_dve.n)
                if hcn + ASL < HC:
                    alpha_issue(hcn + ASL, srcs[hcn + ASL])

        def emit_deferred_rest():
            srcs = [w2n[dcn * 128:(dcn + 1) * 128, p * 1024:(p + 1) * 1024]
                    for dcn in range(DC) for p in range(4)]
            for it in range(ASL):
                alpha_issue(32 + it, srcs[it])
            for j in range(32):
                dcn, piece = j // 4, j % 4
                sl = (32 + j) % ASL
                rec.op("vector",
                       lambda E, sl=sl, p=piece: E.tensor_reduce(
                           patm[:, p:p + 1], pa[:, sl * SLW:(sl + 1) * SLW],
                           axis=AXX, op=ALU.add, apply_absolute_value=True),
                       waits=[(s_al[sl], alpha_loads[32 + j])], inc=cinc(s_dve))
                aslot_gate[sl] = (s_dve, s_dve.n)
                if j + ASL < 32:
                    alpha_issue(32 + j + ASL, srcs[j + ASL])
                if piece == 3:
                    rec.op("vector",
                           lambda E, c=dcn: E.tensor_reduce(
                               alpha2[:, c:c + 1], patm[:], axis=AXX, op=ALU.add),
                           inc=cinc(s_dve))
                    rec.op("vector",
                           lambda E, c=dcn: E.tensor_scalar_mul(
                               alpha2[:, c:c + 1], alpha2[:, c:c + 1], 0.5 / H),
                           inc=cinc(s_dve))
                    a2col_v[dcn] = s_dve.n
            deferred['alpha_done'] = s_dve.n

            wrq = [0] * NSL
            def cast_and_write(write_dst_fn):
                def consume(sl, waits):
                    if wrf[sl]:
                        waits = waits + [(s_scwr[sl], wrf[sl])]
                    if wrq[sl]:
                        waits = waits + [(s_scq[sl], wrq[sl])]
                    rec.op("gpsimd",
                           lambda E, sl=sl: E.tensor_copy(pbs(sl), pfs(sl)),
                           waits=waits, inc=cinc(s_gpe))
                    cv = s_gpe.n
                    rec.op("gpsimdq", write_dst_fn(sl), waits=[(s_gpe, cv)],
                           inc=(s_scq[sl], 16))
                    wrq[sl] = s_scq[sl].inc(16)
                    return (s_gpe, cv)
                return consume

            for kc in range(DC):
                for hh3 in range(3):
                    def _wr(sl, kc=kc, hh3=hh3):
                        return lambda E, sl=sl: E.dma_start(
                            out=scq_o[hh3 * 8:(hh3 + 1) * 8, :, kc, :].rearrange(
                                "o p c -> p o c"),
                            in_=pbs(sl).rearrange(
                                "p (o c) -> p o c", c=128))
                    conv_step(64 + kc * 3 + hh3,
                              wqkv[kc * 128:(kc + 1) * 128,
                                   hh3 * 1024:(hh3 + 1) * 1024],
                              cast_and_write(_wr))
            qkv_scr_wait = [(s, s.n) for s in s_scq]

            for kc in range(DC):
                def _cast(sl, waits, c=kc):
                    rec.op("gpsimd",
                           lambda E, sl=sl, c=c: E.tensor_copy(
                               woutb[:, c * 1024:(c + 1) * 1024], pfs(sl)),
                           waits=waits, inc=cinc(s_gpe))
                    return (s_gpe, s_gpe.n)
                conv_step(88 + kc, wout[kc * 128:(kc + 1) * 128, :], _cast)
            deferred['wout_done'] = s_gpe.n
            deferred['qkv_scr'] = qkv_scr_wait


        def ln_emit(c_, buf, off, W, buf_ready=None):
            bsum, bsq = PB[c_["sbk"][0]], PB[c_["sbk"][1]]
            bcA, bcB = bsum, bsq
            r1row, r2row = c_["r1"], c_["r2"]
            xcrot, sqrot, scr, xn = c_["xc"], c_["sq"], c_["scr"], c_["xn"]
            xc_free, sq_free = c_["xc_free"], c_["sq_free"]
            for c in range(DC):
                sl = c % 2
                w = list(xc_free[sl]) if xc_free[sl] else []
                if c == 0 and buf_ready is not None:
                    w.append(buf_ready)
                rec.op("vector",
                       lambda E, c=c, sl=sl: E.tensor_copy(
                           xcrot[:, sl * MAXE: sl * MAXE + W],
                           buf[:, c * XE + off: c * XE + off + W]),
                       waits=w, inc=cinc(s_dve))
                cast_v = s_dve.n
                w = [(s_dve, cast_v)]
                if sq_free[sl]:
                    w.append((s_pe, sq_free[sl]))
                rec.op("scalar",
                       lambda E, sl=sl: E.square(sqrot[:, sl * MAXE: sl * MAXE + W],
                                                 xcrot[:, sl * MAXE: sl * MAXE + W]),
                       waits=w, inc=cinc(s_act))
                sq_v = s_act.n
                t_mm(bsum[0:1, 0:W], onb[:, 0:1], xcrot[:, sl * MAXE: sl * MAXE + W],
                     start=(c == 0), stop=(c == DC - 1),
                     waits=[(s_dve, cast_v)] + (bank_wait(c_["sbk"][0]) if c == 0 else []),
                     inc=cinc(s_pe))
                xc_free[sl] = [(s_pe, s_pe.n), (s_act, sq_v)]
                t_mm(bsq[0:1, 0:W], onb[:, 0:1], sqrot[:, sl * MAXE: sl * MAXE + W],
                     start=(c == 0), stop=(c == DC - 1),
                     waits=[(s_act, sq_v)] + (bank_wait(c_["sbk"][1]) if c == 0 else []),
                     inc=cinc(s_pe))
                sq_free[sl] = s_pe.n
            stats_mm = s_pe.n
            rec.op("scalar",
                   lambda E: E.activation(r1row[:, 0:W], bsum[0:1, 0:W],
                                          AF.Square, scale=ccs[0:1, 2:3]),
                   waits=[(s_pe, stats_mm)], inc=cinc(s_act))
            m2_v = s_act.n
            rec.op("vector",
                   lambda E: E.scalar_tensor_tensor(r2row[:, 0:W], bsq[0:1, 0:W],
                                                    1.0 / D, r1row[:, 0:W],
                                                    ALU.mult, ALU.subtract),
                   waits=[(s_act, m2_v)], inc=cinc(s_dve))
            var_v = s_dve.n
            rec.op("scalar",
                   lambda E: E.activation(r1row[:, 0:W], r2row[:, 0:W],
                                          AF.Ln, bias=ccs[0:1, 0:1]),
                   waits=[(s_dve, var_v)], inc=cinc(s_act))
            rec.op("scalar",
                   lambda E: E.activation(r2row[:, 0:W], r1row[:, 0:W],
                                          AF.Exp, scale=ccs[0:1, 1:2]),
                   inc=cinc(s_act))
            rstd_v = s_act.n
            rec.op("vector",
                   lambda E: E.scalar_tensor_tensor(r1row[:, 0:W], bsum[0:1, 0:W],
                                                    1.0 / D, r2row[:, 0:W],
                                                    ALU.mult, ALU.mult),
                   waits=[(s_act, rstd_v)], inc=cinc(s_dve))
            mr_v = s_dve.n
            t_mm(bcA[:, 0:W], on1[:], r2row[:, 0:W], True, True,
                 waits=[(s_dve, mr_v)], inc=cinc(s_pe))
            t_mm(bcB[:, 0:W], on1[:], r1row[:, 0:W], True, True,
                 waits=[(s_dve, mr_v)], inc=cinc(s_pe))
            bc_v = s_pe.n
            xn_vc = [0] * DC
            for c in range(DC):
                sl = c % 2
                rec.op("vector",
                       lambda E, c=c, sl=sl: E.tensor_tensor(
                           scr[sl][:, 0:W], buf[:, c * XE + off: c * XE + off + W],
                           bcA[:, 0:W], ALU.mult),
                       waits=[(s_pe, bc_v)] if c == 0 else (),
                       inc=cinc(s_dve))
                w2_ = [(s_pe, c_["xn_rd"])] if (c == 0 and c_["xn_rd"]) else []
                rec.op("vector",
                       lambda E, c=c, sl=sl: E.tensor_tensor(
                           xn[:, c * MAXE: c * MAXE + W], scr[sl][:, 0:W],
                           bcB[:, 0:W], ALU.subtract),
                       waits=w2_, inc=cinc(s_dve))
                xn_vc[c] = s_dve.n
            xn_v = s_dve.n
            bank_free[c_["sbk"][0]] = [(s_dve, xn_v)]
            bank_free[c_["sbk"][1]] = [(s_dve, xn_v)]
            if False:
                yield
            return xn_vc

        w1f = {"v": {0: 0, 1: 0}, "freed": {0: 0, 1: 0}}
        w2f = {"v": {0: 0, 1: 0}, "freed": {0: 0, 1: 0}}
        qkf = {"v": {0: 0, 1: 0}, "freed": {0: 0, 1: 0}}
        last_ffn_resid = [0]

        ffn2_h_emitted = {}

        def ffn_emit(fbh, fbo, xn_c, scr, buf, off, W, first_tile, xn_vc,
                     mid_hook=None, h_flag=None):
            xn = xn_c["xn"]
            evac_last = 0
            for hc in range(HC):
                gidx, slot = hc // 4, (hc // 4) % 2
                if first_tile and hc % 8 == 0:
                    piece = hc // 8
                    for kc in range(DC):
                        def _wr(sl, piece=piece, kc=kc):
                            return lambda E, sl=sl: E.dma_start(
                                out=scw1_g[piece * 2: piece * 2 + 2, :, kc, :].rearrange(
                                    "g p c -> p g c"),
                                in_=pbs(sl).rearrange(
                                    "p (g c) -> p g c", c=512))
                        conv_step(piece * DC + kc,
                                  w1t[kc * 128:(kc + 1) * 128,
                                      piece * 1024:(piece + 1) * 1024],
                                  sign_and_write(_wr))
                    snap = [(s, s.n) for s in s_scwr]
                    w1grp_wait[piece * 2] = snap
                    w1grp_wait[piece * 2 + 1] = snap
                    for kc2 in range(piece * 8, piece * 8 + 8):
                        def _wr2(sl, kc2=kc2):
                            return lambda E, sl=sl: E.dma_start(
                                out=scw2_m[:, :, kc2, :].rearrange("m p c -> p m c"),
                                in_=pbs(sl).rearrange(
                                    "p (m c) -> p m c", c=128))
                        conv_step(32 + kc2, w2t[kc2 * 128:(kc2 + 1) * 128, :],
                                  sign_and_write(_wr2))
                    if piece == 3:
                        deferred['w2_scr'] = [(s, s.n) for s in s_scwr]
                if hc % 4 == 0:
                    w = list(w1grp_wait[gidx]) if first_tile else []
                    if w1f["freed"][slot]:
                        w.append((s_pe, w1f["freed"][slot]))
                    w1f["v"][slot] = sdma(
                        s_fw1[slot],
                        w1s[:, slot * 4096:(slot + 1) * 4096].rearrange(
                            "p (k c) -> p k c", c=512),
                        scw1_g[gidx], waits=w)
                bid = fbh[hc % 2]
                for kc in range(DC):
                    waits = ([(s_fw1[slot], w1f["v"][slot])] + bank_wait(bid)) if kc == 0 else []
                    if hc == 0:
                        waits = waits + [(s_dve, xn_vc[kc])]
                    t_mm(PB[bid][:, 0:W],
                         w1s[:, slot * 4096 + kc * 512 + (hc % 4) * 128:
                             slot * 4096 + kc * 512 + (hc % 4) * 128 + 128],
                         xn[:, kc * MAXE: kc * MAXE + W],
                         start=(kc == 0), stop=(kc == DC - 1),
                         waits=waits, inc=cinc(s_pe) if kc == DC - 1 else None)
                mm_v = s_pe.n
                if hc % 4 == 3:
                    w1f["freed"][slot] = mm_v
                wg = [(s_dve, a1col_v[hc]), (s_pe, mm_v)]
                rec.op("scalar",
                       lambda E, hc=hc, bid=bid: E.activation(
                           g[:, hc * MAXE: hc * MAXE + W], PB[bid][:, 0:W], AF.Gelu,
                           scale=alpha1[:, hc:hc + 1], bias=b1s[:, hc:hc + 1]),
                       waits=wg,
                       inc=cinc(s_act))
                evac_last = s_act.n
                bank_free[bid] = [(s_act, evac_last)]
                yield
            xn_c["xn_rd"] = s_pe.n
            if h_flag is not None:
                ffn2_h_emitted[h_flag] = True
            if mid_hook is not None:
                mid_hook()
            resid_free = {0: 0, 1: 0}
            for mc in range(DC):
                slot = mc % 2
                w = list(deferred['w2_scr']) if first_tile else []
                if w2f["freed"][slot]:
                    w.append((s_pe, w2f["freed"][slot]))
                w2f["v"][slot] = sdma(
                    s_fw2[slot],
                    w2s[:, slot * 4096:(slot + 1) * 4096].rearrange(
                        "p (k c) -> p k c", c=128),
                    scw2_m[mc], waits=w)
                bid = fbo[mc % 2]
                bw = bank_wait(bid)
                for kc2 in range(HC):
                    waits = ([(s_fw2[slot], w2f["v"][slot]), (s_act, evac_last)] + bw) \
                        if kc2 == 0 else []
                    t_mm(PB[bid][:, 0:W],
                         w2s[:, slot * 4096 + kc2 * 128: slot * 4096 + kc2 * 128 + 128],
                         g[:, kc2 * MAXE: kc2 * MAXE + W],
                         start=(kc2 == 0), stop=(kc2 == HC - 1),
                         waits=waits, inc=cinc(s_pe) if kc2 == HC - 1 else None)
                mm_v = s_pe.n
                w2f["freed"][slot] = mm_v
                sl = mc % 2
                w = [(s_dve, a2col_v[mc]), (s_pe, mm_v)]
                if resid_free[sl]:
                    w.append((s_dve, resid_free[sl]))
                rec.op("scalar",
                       lambda E, bid=bid, sl=sl, mc=mc: E.activation(
                           scr[sl][:, 0:W], PB[bid][:, 0:W], AF.Identity,
                           scale=alpha2[:, mc:mc + 1], bias=b2s[:, mc:mc + 1]),
                       waits=w, inc=cinc(s_act))
                ev = s_act.n
                rec.op("vector",
                       lambda E, mc=mc, sl=sl: E.tensor_add(
                           buf[:, mc * XE + off: mc * XE + off + W],
                           buf[:, mc * XE + off: mc * XE + off + W],
                           scr[sl][:, 0:W]),
                       waits=[(s_act, ev)], inc=cinc(s_dve))
                resid_free[sl] = s_dve.n
                bank_free[bid] = [(s_act, ev)]
                yield
            last_ffn_resid[0] = s_dve.n

        phi_done = {0: {}, 1: {}}
        v_done = {0: {}, 1: {}}
        att_done = {}
        qread = {0: 0, 1: 0}
        attnT_rd = [0]

        def qkv_emit(E, first_tile, xn_vc, parity, init_ones=False):
            qphi, kphi, vtok = qphi2[parity], kphi2[parity], vtok2[parity]
            xn, scr = S1c["xn"], S1c["scr"]
            scrw = list(deferred['qkv_scr']) if first_tile else []
            if first_tile:
                scrw.append((s_gpe, deferred['wout_done']))
            qkrot_free = {0: 0, 1: 0}
            phi_free = {0: 0, 1: 0}
            scr_free = {0: 0, 1: 0}
            qk_ev = {}

            def qk_part(oc):
                slot = oc % 2
                w = list(scrw)
                if qkf["freed"][slot]:
                    w.append((s_pe, qkf["freed"][slot]))
                qkf["v"][slot] = sdma(
                    s_fqk[slot],
                    qks[:, slot * 1024:(slot + 1) * 1024].rearrange(
                        "p (k c) -> p k c", c=128),
                    scq_o[oc], waits=w)
                bid = 2 + slot
                bw = bank_wait(bid)
                for kc in range(DC):
                    waits = ([(s_fqk[slot], qkf["v"][slot])] + bw) if kc == 0 else []
                    if oc == 0:
                        waits = waits + [(s_dve, xn_vc[kc])]
                    t_mm(PB[bid][:, 0:E],
                         qks[:, slot * 1024 + kc * 128: slot * 1024 + kc * 128 + 128],
                         xn[:, kc * MAXE: kc * MAXE + E],
                         start=(kc == 0), stop=(kc == DC - 1),
                         waits=waits, inc=cinc(s_pe) if kc == DC - 1 else None)
                mm_v = s_pe.n
                qkf["freed"][slot] = mm_v
                w = [(s_pe, mm_v)]
                if qkrot_free[slot]:
                    w.append((s_pe, qkrot_free[slot]))
                rec.op("scalar",
                       lambda E_, sl=slot, bid=bid: E_.copy(
                           qkrot[:, sl * MAXE: sl * MAXE + E], PB[bid][:, 0:E]),
                       waits=w, inc=cinc(s_act))
                qk_ev[oc] = s_act.n
                bank_free[bid] = [(s_act, qk_ev[oc])]

            def phi_part(oc):
                slot = oc % 2
                pbid = 0 + slot
                head = oc % 8
                w = [(s_act, qk_ev[oc])] + bank_wait(pbid)
                if phi_free[slot]:
                    w.append((s_dve, phi_free[slot]))
                w.append((s_gpe, projb_v))
                t_mm(PB[pbid][0:FEAT, 0:E], projb[:],
                     qkrot[:, slot * MAXE: slot * MAXE + E],
                     True, True, waits=w, inc=cinc(s_pe))
                phi_v = s_pe.n
                qkrot_free[slot] = phi_v
                rows = (head % 2) * 64
                w = [(s_pe, phi_v)]
                if scr_free[slot]:
                    w.append((s_dve, scr_free[slot]))
                rec.op("scalar",
                       lambda E_, pbid=pbid, rows=rows: E_.activation(
                           scr[0][rows:rows + 64, 0:E], PB[pbid][0:FEAT, 0:E], AF.Relu),
                       waits=w, inc=cinc(s_act))
                r_v = s_act.n
                rec.op("vector",
                       lambda E_, pbid=pbid, rows=rows: E_.tensor_sub(
                           scr[1][rows:rows + 64, 0:E], PB[pbid][0:FEAT, 0:E],
                           scr[0][rows:rows + 64, 0:E]),
                       waits=[(s_act, r_v)], inc=cinc(s_dve))
                m_v = s_dve.n
                phi_free[slot] = m_v
                bank_free[pbid] = [(s_dve, m_v)]
                rec.op("scalar",
                       lambda E_, rows=rows: E_.activation(
                           scr[2][rows:rows + 64, 0:E], scr[1][rows:rows + 64, 0:E],
                           AF.Exp),
                       waits=[(s_dve, m_v)], inc=cinc(s_act))
                e_v = s_act.n
                dst = qphi if oc < 8 else kphi
                pair = head // 2
                w = [(s_act, e_v)]
                if oc in (0, 8) and qread[parity]:
                    w.append((s_pe, qread[parity]))
                rec.op("vector",
                       lambda E_, dst=dst, pair=pair, rows=rows: E_.tensor_add(
                           dst[rows:rows + 64, pair * MAXE: pair * MAXE + E],
                           scr[2][rows:rows + 64, 0:E], scr[0][rows:rows + 64, 0:E]),
                       waits=w, inc=cinc(s_dve))
                scr_free[slot] = s_dve.n
                phi_done[parity][oc] = s_dve.n

            for step in range(17):
                if step < 16:
                    qk_part(step)
                if step >= 1:
                    phi_part(step - 1)
                yield
            S1c["xn_rd"] = s_pe.n
            if init_ones:
                for h in range(HEADS):
                    for tc in range(4):
                        col = (h * 4 + tc) * VST + 128
                        w = [(s_dve, deferred['alpha_done'])] if (h == 0 and tc == 0) else []
                        rec.op("gpsimd",
                               lambda E_, c=col: E_.memset(vtok[:, c:c + 1], 1.0),
                               waits=w, inc=cinc(s_gpe))
                v_ones[parity] = s_gpe.n
            nch = (E + 127) // 128
            vr_free = {0: 0, 1: 0}
            first_copy = True
            for h2 in range(2):
                vmm_v = 0
                for kc in range(DC):
                    slot = kc % 2
                    w = list(scrw)
                    if vr_free[slot]:
                        w.append((s_pe, vr_free[slot]))
                    v_l = sdma(s_fvr[slot],
                               vrs[:, slot * 512:(slot + 1) * 512].rearrange(
                                   "p (o c) -> p o c", c=128),
                               scq_o[16 + h2 * 4: 16 + h2 * 4 + 4, :, kc, :].rearrange(
                                   "o p c -> p o c"),
                               waits=w)
                    for tc in range(nch):
                        rows = min(128, E - tc * 128)
                        bid = tc
                        waits = [(s_fvr[slot], v_l)] + (bank_wait(bid) if kc == 0 else [])
                        t_mm(PB[bid][0:rows, 0:512],
                             xn[:, kc * MAXE + tc * 128: kc * MAXE + tc * 128 + rows],
                             vrs[:, slot * 512:(slot + 1) * 512],
                             start=(kc == 0), stop=(kc == DC - 1),
                             waits=waits,
                             inc=cinc(s_pe) if tc == nch - 1 else None)
                    vr_free[slot] = s_pe.n
                    yield
                vmm_v = s_pe.n
                for tc in range(nch):
                    rows = min(128, E - tc * 128)
                    bid = tc
                    for hh in range(4):
                        head = h2 * 4 + hh
                        col = (head * 4 + tc) * VST
                        w = [(s_pe, vmm_v)]
                        if first_copy:
                            first_copy = False
                            if qread[parity]:
                                w.append((s_pe, qread[parity]))
                            if init_ones:
                                w.append((s_gpe, v_ones[parity]))
                        rec.op("scalar",
                               lambda E_, bid=bid, rows=rows, hh=hh, col=col, vtok=vtok:
                               E_.copy(vtok[0:rows, col:col + 128],
                                       PB[bid][0:rows, hh * 128: hh * 128 + 128]),
                               waits=w, inc=cinc(s_act))
                    bank_free[bid] = [(s_act, s_act.n)]
                v_done[parity][h2] = s_act.n
            S1c["xn_rd"] = s_pe.n

        attn_state = {"smA_rd": {0: 0, 1: 0}, "an_rd": {0: 0, 1: 0},
                      "bankA": {0: [], 1: []}, "bankO": {0: [], 1: []},
                      "pt": {0: 0, 1: 0}, "an_hist": {0: 0, 1: 0}}

        def attn_emit(T, E, first, last, parity):
            qphi, kphi, vtok = qphi2[parity], kphi2[parity], vtok2[parity]
            nqb = (T + 127) // 128
            N = nqb * HEADS
            smA_rd, an_rd = attn_state["smA_rd"], attn_state["an_rd"]
            bankA, bankO = attn_state["bankA"], attn_state["bankO"]
            pt_free = attn_state["pt"]
            st = [dict() for _ in range(N)]
            first_cp = [True]

            def geom(c):
                m, head = c // HEADS, c % HEADS
                qb = min(128, T - m * 128)
                return m, head, qb, min(128, E - m * 128), max(0, qb - 122)

            def stage(s, c):
                m, head, qb, arows, brows = geom(c)
                slot = c % 2
                bid, bidT = 4 + slot, 6 + slot
                pair, rows = head // 2, (head % 2) * 64
                S = st[c]
                if s == 0:
                    w = list(bankA[slot]) + bank_wait(bid)
                    if m == 0:
                        w += [(s_dve, phi_done[parity][head]),
                              (s_dve, phi_done[parity][8 + head])]
                    t_mm(PB[bid][0:arows, 0:qb],
                         kphi[rows:rows + 64,
                              pair * MAXE + m * 128: pair * MAXE + m * 128 + arows],
                         qphi[rows:rows + 64,
                              pair * MAXE + 3 + m * 128: pair * MAXE + 3 + m * 128 + qb],
                         True, True, waits=w,
                         inc=cinc(s_pe) if brows == 0 else None)
                    if brows > 0:
                        t_mm(PB[bid][0:brows, 128:128 + qb],
                             kphi[rows:rows + 64,
                                  pair * MAXE + (m + 1) * 128:
                                  pair * MAXE + (m + 1) * 128 + brows],
                             qphi[rows:rows + 64,
                                  pair * MAXE + 3 + m * 128:
                                  pair * MAXE + 3 + m * 128 + qb],
                             True, True, waits=(), inc=cinc(s_pe))
                    S["sc"] = s_pe.n
                elif s == 1:
                    mvar = 0 if (first and m == 0) else (2 if (last and m == nqb - 1) else 1)
                    w = [(s_pe, S["sc"])]
                    if smA_rd[slot]:
                        w.append((s_pe, smA_rd[slot]))
                    rec.op("vector",
                           lambda E_, bid=bid, slot=slot, arows=arows, qb=qb, mvar=mvar:
                           E_.tensor_tensor(
                               smA[0:arows, slot * 128: slot * 128 + qb],
                               PB[bid][0:arows, 0:qb],
                               mA[0:arows, mvar * 128: mvar * 128 + qb], ALU.mult),
                           waits=w,
                           inc=cinc(s_dve) if brows == 0 else None)
                    if brows > 0:
                        rec.op("vector",
                               lambda E_, bid=bid, slot=slot, brows=brows, qb=qb:
                               E_.tensor_tensor(
                                   smB[0:brows, slot * 128: slot * 128 + qb],
                                   PB[bid][0:brows, 128:128 + qb],
                                   mB[0:brows, 0:qb], ALU.mult),
                               inc=cinc(s_dve))
                    S["sm"] = s_dve.n
                    bankA[slot] = [(s_dve, S["sm"])]
                elif s == 2:
                    w = [(s_dve, S["sm"]), (s_gpe, v_ones[parity]),
                         (s_act, v_done[parity][head // 4])] + list(bankO[slot])
                    t_mm(PB[bid][0:qb, 256:385],
                         smA[0:arows, slot * 128: slot * 128 + qb],
                         vtok[0:arows, (head * 4 + m) * VST: (head * 4 + m) * VST + 129],
                         True, (brows == 0), waits=w,
                         inc=cinc(s_pe) if brows == 0 else None)
                    if brows > 0:
                        t_mm(PB[bid][0:qb, 256:385],
                             smB[0:brows, slot * 128: slot * 128 + qb],
                             vtok[0:brows,
                                  (head * 4 + m + 1) * VST: (head * 4 + m + 1) * VST + 129],
                             False, True, waits=(), inc=cinc(s_pe))
                    S["at"] = s_pe.n
                    smA_rd[slot] = S["at"]
                elif s == 3:
                    w = [(s_pe, S["at"])]
                    an_prev = st[c - 2]["an"] if c >= 2 else attn_state["an_hist"][slot]
                    if an_prev:
                        w.append((s_act, an_prev))
                    rec.op("vector",
                           lambda E_, bid=bid, qb=qb, slot=slot: E_.tensor_scalar(
                               zcol[0:qb, 2 * slot: 2 * slot + 1],
                               PB[bid][0:qb, 384:385], EPS_Z, None, ALU.add),
                           waits=w)
                    rec.op("vector",
                           lambda E_, qb=qb, slot=slot: E_.reciprocal(
                               zcol[0:qb, 2 * slot + 1: 2 * slot + 2],
                               zcol[0:qb, 2 * slot: 2 * slot + 1]),
                           inc=cinc(s_dve))
                    S["zr"] = s_dve.n
                elif s == 4:
                    w = [(s_dve, S["zr"])]
                    if an_rd[slot]:
                        w.append((s_pe, an_rd[slot]))
                    rec.op("scalar",
                           lambda E_, bid=bid, qb=qb, slot=slot: E_.activation(
                               attn_n[0:qb, slot * 128: slot * 128 + 128],
                               PB[bid][0:qb, 256:384], AF.Identity,
                               scale=zcol[0:qb, 2 * slot + 1: 2 * slot + 2]),
                           waits=w, inc=cinc(s_act))
                    S["an"] = s_act.n
                    bankO[slot] = [(s_act, S["an"])]
                elif s == 5:
                    w = [(s_act, S["an"])]
                    if pt_free[slot]:
                        w.append((s_act, pt_free[slot]))
                    rec.op("tensor",
                           lambda E_, bidT=bidT, slot=slot, qb=qb: E_.transpose(
                               PB[bidT][0:128, 0:qb],
                               attn_n[0:qb, slot * 128: slot * 128 + 128],
                               id32[0:qb, 0:qb]),
                           waits=w, inc=cinc(s_pe))
                    S["tr"] = s_pe.n
                    an_rd[slot] = S["tr"]
                else:
                    w = [(s_pe, S["tr"])]
                    if first_cp[0]:
                        first_cp[0] = False
                        if attnT_rd[0]:
                            w.append((s_pe, attnT_rd[0]))
                        else:
                            w += list(deferred['qkv_scr'])
                    rec.op("scalar",
                           lambda E_, bidT=bidT, head=head, m=m, qb=qb: E_.copy(
                               attnT[:, head * MAXE + m * 128: head * MAXE + m * 128 + qb],
                               PB[bidT][0:128, 0:qb]),
                           waits=w, inc=cinc(s_act))
                    pt_free[slot] = s_act.n
                    if m == nqb - 1:
                        att_done[head] = s_act.n

            GROUPS = [(0,), (1, 2, 3), (4, 5, 6)]
            for t in range(N + len(GROUPS) - 1):
                for gi in range(len(GROUPS) - 1, -1, -1):
                    c = t - gi
                    if 0 <= c < N:
                        for s in GROUPS[gi]:
                            stage(s, c)
                yield
            qread[parity] = s_pe.n
            bank_free[4] = bankA[0] + bankO[0]
            bank_free[5] = bankA[1] + bankO[1]
            bank_free[6] = [(s_act, pt_free[0])]
            bank_free[7] = [(s_act, pt_free[1])]
            attn_state["an_hist"][0] = st[N - 2]["an"]
            attn_state["an_hist"][1] = st[N - 1]["an"]

        def outproj_emit(buf, T):
            scr = S2c["scr"]
            resid_free = {0: 0, 1: 0}
            for mc in range(DC):
                bid = 4 + (mc % 2)
                bw = bank_wait(bid)
                for kc in range(DC):
                    waits = ([(s_gpe, deferred['wout_done'])] + bw) if kc == 0 else []
                    if mc == 0:
                        waits = waits + [(s_act, att_done[kc])]
                    t_mm(PB[bid][:, 0:T],
                         woutb[:, kc * 1024 + mc * 128: kc * 1024 + mc * 128 + 128],
                         attnT[:, kc * MAXE: kc * MAXE + T],
                         start=(kc == 0), stop=(kc == DC - 1),
                         waits=waits, inc=cinc(s_pe) if kc == DC - 1 else None)
                mm_v = s_pe.n
                sl = mc % 2
                w = [(s_pe, mm_v)]
                if resid_free[sl]:
                    w.append((s_dve, resid_free[sl]))
                rec.op("scalar",
                       lambda E_, bid=bid, sl=sl, mc=mc, T=T: E_.activation(
                           scr[sl][:, 0:T], PB[bid][:, 0:T], AF.Identity,
                           bias=bouts[:, mc:mc + 1]),
                       waits=w, inc=cinc(s_act))
                ev = s_act.n
                rec.op("vector",
                       lambda E_, mc=mc, sl=sl, T=T: E_.tensor_add(
                           buf[:, mc * XE + 3: mc * XE + 3 + T],
                           buf[:, mc * XE + 3: mc * XE + 3 + T], scr[sl][:, 0:T]),
                       waits=[(s_act, ev)], inc=cinc(s_dve))
                resid_free[sl] = s_dve.n
                bank_free[bid] = [(s_act, ev)]
                yield
            attnT_rd[0] = s_pe.n

        def emit_store(t, resid_v):
            for c in range(DC):
                gdma(s_out, yT[c * 128:(c + 1) * 128, offs[t]: offs[t] + TILES[t]],
                     xb[t % 2][:, c * XE + 3: c * XE + 3 + TILES[t]],
                     waits=[(s_dve, resid_v)] if c == 0 else ())
            out_v[t] = s_out.n

        xnb_vc = {}
        ffn2_resid = {}

        def s1_window(j):
            if j >= 1:
                t = j - 1
                yield from ffn_emit((2, 3), (2, 3), S2c, scrA, xb[t % 2], 3,
                                    TILES[t], False, xnb_vc[t], h_flag=t)
                ffn2_resid[t] = last_ffn_resid[0]
                emit_store(t, ffn2_resid[t])
            if j + 1 < NT:
                t2 = j + 1
                if t2 >= 2:
                    load_x(t2)
                yield
                E2 = TILES[t2] + 6
                xv = yield from ln_emit(S1c, xb[t2 % 2], 0, E2,
                                        buf_ready=xload_v[t2])
                yield from ffn_emit((2, 3), (2, 3), S1c, scrA, xb[t2 % 2], 0,
                                    E2, False, xv)
                xv = yield from ln_emit(S1c, xb[t2 % 2], 0, E2)
                yield from qkv_emit(E2, False, xv, parity=t2 % 2,
                                    init_ones=(t2 == 1))

        def s2_window(j):
            E = TILES[j] + 6
            yield from attn_emit(TILES[j], E, first=(j == 0), last=(j == NT - 1),
                                 parity=j % 2)
            yield from outproj_emit(xb[j % 2], TILES[j])
            while j >= 1 and not ffn2_h_emitted.get(j - 1):
                yield
            xnb_vc[j] = yield from ln_emit(S2c, xb[j % 2], 3, TILES[j])
            yield

        def drain(gen):
            while True:
                try:
                    next(gen)
                except StopIteration as e:
                    return e.value

        def interleave(gb, ga, head=6, ga_per_gb=3):
            for _ in range(head):
                if next(gb, StopIteration) is StopIteration:
                    gb = None
                    break
            alive = [x for x in (gb, ga) if x is not None]
            while alive:
                if gb in alive:
                    if next(gb, StopIteration) is StopIteration:
                        alive.remove(gb)
                if ga in alive:
                    for _ in range(ga_per_gb):
                        if next(ga, StopIteration) is StopIteration:
                            alive.remove(ga)
                            break

        for e in ("tensor", "scalar", "vector"):
            rec.op(e, None, waits=[xload_v[0], (s_const, consts_v)])
        E0 = TILES[0] + 6
        xv0 = drain(ln_emit(S1c, xb[0], 0, E0))
        emit_alpha1()
        drain(ffn_emit((2, 3), (2, 3), S1c, scrA, xb[0], 0, E0, True, xv0,
                       mid_hook=emit_deferred_rest))
        xv0 = drain(ln_emit(S1c, xb[0], 0, E0))
        drain(qkv_emit(E0, True, xv0, parity=0))

        for j in range(NT):
            interleave(s2_window(j), s1_window(j))

        t = NT - 1
        drain(ffn_emit((2, 3), (2, 3), S2c, scrA, xb[t % 2], 3,
                       TILES[t], False, xnb_vc[t]))
        emit_store(t, last_ffn_resid[0])

        rec.op("gpsimd", None, waits=[(s_out, out_v[NT - 1])])

        block = ctx.enter_context(nc.Block())

        def emit_for(name):
            extra = {"gpsimd": {"gpsimdq"}, "scalar": {"scalarq"}}
            names = {name} | extra.get(name, set())

            def emitter(engobj):
                for eng, fn, waits, incs in rec.ops:
                    if eng not in names:
                        continue
                    for s, v in waits:
                        engobj.wait_ge(s.h, v)
                    if fn is None:
                        if incs:
                            inst = engobj.nop()
                            for s, amt in incs:
                                inst.then_inc(s.h, amt)
                        continue
                    inst = fn(engobj)
                    for s, amt in incs:
                        inst.then_inc(s.h, amt)
            return emitter

        block.tensor(emit_for("tensor"))
        block.scalar(emit_for("scalar"))
        block.vector(emit_for("vector"))
        block.sync(emit_for("sync"))
        block.gpsimd(emit_for("gpsimd"))

    return nc


def _host_inputs(L0_attn_x, L0_attn_w_qkv, L0_attn_proj, L0_attn_w_out, L0_attn_b_out,
                 L0_ffn_w1, L0_ffn_b1, L0_ffn_w2, L0_ffn_b2):
    x = np.asarray(L0_attn_x, dtype=np.float32)
    wqkv = np.ascontiguousarray(np.asarray(L0_attn_w_qkv, np.float32))
    projm = np.ascontiguousarray(np.asarray(L0_attn_proj, np.float32))
    woutm = np.ascontiguousarray(np.asarray(L0_attn_w_out, np.float32))
    bout = np.asarray(L0_attn_b_out, np.float32)
    w1 = np.asarray(L0_ffn_w1, np.float32)
    b1 = np.asarray(L0_ffn_b1, np.float32)
    w2 = np.asarray(L0_ffn_w2, np.float32)
    b2 = np.asarray(L0_ffn_b2, np.float32)

    w1t = np.ascontiguousarray(w1.T)
    w2t = np.ascontiguousarray(w2.T)
    b1cm = np.ascontiguousarray(b1.reshape(HC, 128).T)
    b2cm = np.ascontiguousarray(b2.reshape(DC, 128).T)
    boutcm = np.ascontiguousarray(bout.reshape(DC, 128).T)
    identf = np.eye(128, dtype=np.float32)
    onesb = np.ones((128, 1), dtype=ml_dtypes.bfloat16)
    ones1 = np.ones((1, 128), dtype=np.float32)
    cconst = np.broadcast_to(np.array([EPS_LN, -0.5, 1.0 / D], np.float32),
                             (128, 3)).copy()

    j = np.arange(128)[:, None]
    q = np.arange(128)[None, :]
    bandA = ((j - q >= 0) & (j - q <= 6)).astype(np.float32)
    jb = np.arange(6)[:, None]
    bandB = (q >= 122 + jb).astype(np.float32)
    T_last = TILES[-1]
    cut_last = (T_last + 3) - 384

    in_maps = []
    for core in range(8):
        b = core // 4
        start = (core % 4) * NTOK
        lo, hi = start - 3, start + NTOK + 3
        xs = np.zeros((NTOK + 6, D), np.float32)
        clo, chi = max(lo, 0), min(hi, 16384)
        xs[clo - lo: chi - lo, :] = x[b, clo:chi, :]
        xTm = np.ascontiguousarray(xs.T)

        Afirst = bandA.copy()
        Alast = bandA.copy()
        if start == 0:
            Afirst[0:3, :] = 0.0
        if start + NTOK == 16384:
            Alast[cut_last:cut_last + 3, :] = 0.0
        maskA3m = np.ascontiguousarray(np.concatenate([Afirst, bandA, Alast], axis=1))

        in_maps.append({
            "xT": xTm, "wqkv": wqkv, "wout": woutm, "proj": projm,
            "w1t": w1t, "w1n": w1, "w2t": w2t, "w2n": w2,
            "b1c": b1cm, "b2c": b2cm, "boutc": boutcm,
            "maskA3": maskA3m, "maskB": np.ascontiguousarray(bandB),
            "identf": identf, "onesb": onesb, "ones1": ones1, "cconst": cconst,
        })
    return in_maps


def kernel(L0_attn_x, L0_attn_w_qkv, L0_attn_proj, L0_attn_w_out, L0_attn_b_out,
           L0_ffn_w1, L0_ffn_b1, L0_ffn_w2, L0_ffn_b2):
    in_maps = _host_inputs(L0_attn_x, L0_attn_w_qkv, L0_attn_proj, L0_attn_w_out,
                           L0_attn_b_out, L0_ffn_w1, L0_ffn_b1, L0_ffn_w2, L0_ffn_b2)
    if "nc" not in _CACHED:
        _CACHED["nc"] = build_nc()
    nc = _CACHED["nc"]
    res = run_bass_kernel_spmd(nc, in_maps, core_ids=list(range(8)))
    out = np.empty((2, 16384, D), np.float32)
    for core in range(8):
        b = core // 4
        start = (core % 4) * NTOK
        out[b, start:start + NTOK, :] = res.results[core]["yT"].T
    return out


# version 32 (final)
# speedup vs baseline: 1.1038x; 1.1038x over previous
import numpy as np
import ml_dtypes
from contextlib import ExitStack

import concourse.bass as bass
import concourse.mybir as mybir
from concourse.bass_utils import run_bass_kernel_spmd

F32 = mybir.dt.float32
BF16 = mybir.dt.bfloat16
AF = mybir.ActivationFunctionType
ALU = mybir.AluOpType
AXX = mybir.AxisListType.X

D = 1024
DC = 8
H = 4096
HC = 32
HEADS = 8
DH = 128
FEAT = 64
NTOK = 4096
TILES = [456] * 8 + [448]
NT = len(TILES)
MAXE = 462
XE = 464
VST = 130
assert sum(TILES) == NTOK
EPS_LN = 1e-5
EPS_Z = 1e-8

_CACHED = {}


class SemC:
    def __init__(self, name):
        self.name = name
        self.h = None
        self.n = 0

    def inc(self, amt=1):
        self.n += amt
        return self.n


class Rec:
    def __init__(self, eng_sems=None):
        self.ops = []
        self.eng_sems = eng_sems or {}

    def op(self, eng, fn=None, waits=(), inc=None):
        incs = ()
        if inc is not None:
            incs = tuple(inc) if isinstance(inc, list) else (inc,)
        waits = list(waits)
        own = self.eng_sems.get(eng)
        if own is not None and fn is not None:
            own_amt = sum(a for (s, a) in incs if s is own)
            if own_amt == 0:
                own.inc(1)
                incs = incs + ((own, 1),)
                own_amt = 1
            waits.append((own, own.n - own_amt))
        waits = tuple((s, v) for (s, v) in waits if v > 0)
        for s, v in waits:
            assert v <= s.n, f"wait-on-future: eng={eng} sem={s.name} v={v} planned={s.n}"
        self.ops.append((eng, fn, waits, incs))


def cinc(sem):
    sem.inc(1)
    return (sem, 1)


def build_nc():
    nc = bass.Bass("TRN2", target_bir_lowering=False, debug=False)

    xT = nc.dram_tensor("xT", [D, NTOK + 6], F32, kind="ExternalInput").ap()
    wqkv = nc.dram_tensor("wqkv", [D, 3 * D], F32, kind="ExternalInput").ap()
    wout = nc.dram_tensor("wout", [D, D], F32, kind="ExternalInput").ap()
    proj = nc.dram_tensor("proj", [DH, FEAT], F32, kind="ExternalInput").ap()
    w1t = nc.dram_tensor("w1t", [D, H], F32, kind="ExternalInput").ap()
    w1n = nc.dram_tensor("w1n", [H, D], F32, kind="ExternalInput").ap()
    w2t = nc.dram_tensor("w2t", [H, D], F32, kind="ExternalInput").ap()
    w2n = nc.dram_tensor("w2n", [D, H], F32, kind="ExternalInput").ap()
    b1c = nc.dram_tensor("b1c", [128, HC], F32, kind="ExternalInput").ap()
    b2c = nc.dram_tensor("b2c", [128, DC], F32, kind="ExternalInput").ap()
    boutc = nc.dram_tensor("boutc", [128, DC], F32, kind="ExternalInput").ap()
    maskA3 = nc.dram_tensor("maskA3", [128, 3 * 128], F32, kind="ExternalInput").ap()
    maskB = nc.dram_tensor("maskB", [6, 128], F32, kind="ExternalInput").ap()
    identf = nc.dram_tensor("identf", [128, 128], F32, kind="ExternalInput").ap()
    onesb = nc.dram_tensor("onesb", [128, 1], BF16, kind="ExternalInput").ap()
    ones1 = nc.dram_tensor("ones1", [1, 128], F32, kind="ExternalInput").ap()
    cconst = nc.dram_tensor("cconst", [128, 3], F32, kind="ExternalInput").ap()
    yT = nc.dram_tensor("yT", [D, NTOK], F32, kind="ExternalOutput").ap()

    scw1 = nc.dram_tensor("scw1", [8 * 128 * DC, 512], BF16, kind="Internal").ap()
    scw2 = nc.dram_tensor("scw2", [DC * 128 * HC, 128], BF16, kind="Internal").ap()
    scqkv = nc.dram_tensor("scqkv", [24 * 128 * DC, 128], BF16, kind="Internal").ap()

    s_pe = SemC("s_pe")
    s_act = SemC("s_act")
    s_dve = SemC("s_dve")
    s_gpe = SemC("s_gpe")
    s_const = SemC("s_const")
    s_wload = [SemC(f"s_wl{i}") for i in range(4)]
    s_scwr = [SemC(f"s_scwr{i}") for i in range(4)]
    s_al = [SemC(f"s_al{i}") for i in range(4)]
    s_scq = [SemC(f"s_scq{i}") for i in range(4)]
    s_fw1 = [SemC("s_fw1a"), SemC("s_fw1b")]
    s_fw2 = [SemC("s_fw2a"), SemC("s_fw2b")]
    s_fqk = [SemC("s_fqka"), SemC("s_fqkb")]
    s_fvr = [SemC("s_fvra"), SemC("s_fvrb")]
    s_x = [SemC("s_x0"), SemC("s_x1")]
    s_out = SemC("s_out")
    sems = ([s_pe, s_act, s_dve, s_gpe, s_const, s_out]
            + s_wload + s_scwr + s_al + s_scq
            + s_fw1 + s_fw2 + s_fqk + s_fvr + s_x)
    rec = Rec(eng_sems={"scalar": s_act, "vector": s_dve, "gpsimd": s_gpe})

    with ExitStack() as ctx:
        sb = lambda n, s, d: ctx.enter_context(nc.sbuf_tensor(n, s, d))
        psa = lambda n, s, d: ctx.enter_context(nc.psum_tensor(n, s, d))

        xb = [sb("xb0", [128, DC * XE], F32), sb("xb1", [128, DC * XE], F32)]
        xn_a = sb("xn_a", [128, DC * MAXE], BF16)
        xn_b = sb("xn_b", [128, DC * MAXE], BF16)
        xc_a = sb("xc_a", [128, 2 * MAXE], BF16)
        sq_a = sb("sq_a", [128, 2 * MAXE], BF16)
        xc_b = sb("xc_b", [128, 2 * MAXE], BF16)
        sq_b = sb("sq_b", [128, 2 * MAXE], BF16)
        scrA = [sb(f"scrA{i}", [128, MAXE], F32) for i in range(3)]
        scrB = [sb(f"scrB{i}", [128, MAXE], F32) for i in range(3)]
        rows_t = sb("rows_t", [1, 4 * MAXE], F32)
        g = sb("g", [128, HC * MAXE], BF16)
        qkrot = sb("qkrot", [128, 2 * MAXE], BF16)
        qphi2 = [sb("qphi0", [128, 4 * MAXE], BF16), sb("qphi1", [128, 4 * MAXE], BF16)]
        kphi2 = [sb("kphi0", [128, 4 * MAXE], BF16), sb("kphi1", [128, 4 * MAXE], BF16)]
        vtok2 = [sb("vtok0", [128, HEADS * 4 * VST], BF16),
                 sb("vtok1", [128, HEADS * 4 * VST], BF16)]
        smA = sb("smA", [128, 2 * 128], BF16)
        smB = sb("smB", [6, 2 * 128], BF16)
        attn_n = sb("attn_n", [128, 2 * 128], F32)
        attnT = sb("attnT", [128, HEADS * MAXE], BF16)
        zcol = sb("zcol", [128, 4], F32)
        w1s = sb("w1s", [128, 2 * 4096], BF16)
        w2s = sb("w2s", [128, 2 * 4096], BF16)
        qks = sb("qks", [128, 2 * 1024], BF16)
        vrs = sb("vrs", [128, 2 * 512], BF16)
        woutb = sb("woutb", [128, DC * 1024], BF16)
        projb = sb("projb", [128, FEAT], BF16)
        alpha1 = sb("alpha1", [128, HC], F32)
        alpha2 = sb("alpha2", [128, DC], F32)
        b1s = sb("b1s", [128, HC], F32)
        b2s = sb("b2s", [128, DC], F32)
        bouts = sb("bouts", [128, DC], F32)
        mA = sb("mA", [128, 3 * 128], F32)
        mB = sb("mB", [6, 128], F32)
        id32 = sb("id32", [128, 128], F32)
        onb = sb("onb", [128, 1], BF16)
        on1 = sb("on1", [1, 128], F32)
        ccs = sb("ccs", [128, 3], F32)
        pf32 = sb("pf32", [128, 2 * 1024], F32)
        patm = sb("patm", [128, 4], F32)

        def overlay(name, shape, dtype, host):
            hnd = nc.virtual_sbuf_tensor(name, shape, dtype)
            nc._tensor(hnd.name, hnd.shape, hnd.dtype, type="SB")
            mloc = nc.lookup_mloc(hnd)
            mloc.allocated = True
            mloc.addr = nc.lookup_mloc(host).addr
            return hnd

        pa = overlay("pa", [128, 2 * 1024], F32, vtok2[1])
        pbf = overlay("pbf", [128, 3 * 1024], BF16, attnT)
        pf32c = overlay("pf32c", [128, 1024], F32, xn_b)

        SLW = 1024
        PB = [psa(f"PB{i}", [128, 512], F32) for i in range(8)]

        r1a = rows_t[0:1, 0:MAXE]
        r2a = rows_t[0:1, MAXE:2 * MAXE]
        r1b = rows_t[0:1, 2 * MAXE:3 * MAXE]
        r2b = rows_t[0:1, 3 * MAXE:4 * MAXE]

        for s in sems:
            s.h = ctx.enter_context(nc.semaphore(s.name))

        bank_free = {}

        def bank_wait(bid):
            w = bank_free.get(bid)
            return list(w) if w else []

        def t_mm(out, lhsT, rhs, start, stop, waits=(), inc=None):
            rec.op("tensor",
                   lambda E, o=out, l=lhsT, r=rhs, st=start, sp=stop:
                   E.matmul(o, l, r, start=st, stop=sp),
                   waits=waits, inc=inc)

        def gdma(sem, dst, src, waits=()):
            v = sem.inc(16)
            rec.op("gpsimdq", lambda E, d=dst, s=src: E.dma_start(out=d, in_=s),
                   waits=waits, inc=(sem, 16))
            return v

        def sdma(sem, dst, src, waits=()):
            v = sem.inc(16)
            rec.op("sync", lambda E, d=dst, s=src: E.dma_start(out=d, in_=s),
                   waits=waits, inc=(sem, 16))
            return v

        S1c = {"xn": xn_a, "xc": xc_a, "sq": sq_a, "scr": scrA,
               "r1": r1a, "r2": r2a, "sbk": (0, 1),
               "xc_free": [None, None], "sq_free": [0, 0], "xn_rd": 0}
        S2c = {"xn": xn_b, "xc": xc_b, "sq": sq_b, "scr": scrB,
               "r1": r1b, "r2": r2b, "sbk": (4, 5),
               "xc_free": [None, None], "sq_free": [0, 0], "xn_rd": 0}

        offs = [sum(TILES[:i]) for i in range(NT)]
        xload_v = {}
        out_v = {}

        def load_x(i):
            buf = xb[i % 2]
            E = TILES[i] + 6
            waits = [] if i < 2 else [(s_out, out_v[i - 2])]
            sem = s_x[i % 2]
            for c in range(DC):
                gdma(sem, buf[:, c * XE: c * XE + E],
                     xT[c * 128:(c + 1) * 128, offs[i]: offs[i] + E],
                     waits=waits if c == 0 else ())
            xload_v[i] = (sem, sem.n)

        load_x(0)
        load_x(1)

        gdma(s_const, mA[:], maskA3)
        gdma(s_const, mB[:], maskB)
        gdma(s_const, id32[:], identf)
        gdma(s_const, onb[:], onesb)
        gdma(s_const, on1[:], ones1)
        gdma(s_const, ccs[:], cconst)
        gdma(s_const, b1s[:], b1c)
        gdma(s_const, b2s[:], b2c)
        gdma(s_const, bouts[:], boutc)
        gdma(s_const, pf32[:, 0:FEAT], proj)
        consts_v = s_const.n
        rec.op("gpsimd", lambda E: E.tensor_copy(projb[:], pf32[:, 0:FEAT]),
               waits=[(s_const, consts_v)], inc=cinc(s_gpe))
        projb_v = s_gpe.n
        projb_gate = (s_gpe, projb_v)
        rec.op("gpsimd", lambda E: E.tensor_scalar_mul(b2s[:], b2s[:], 0.5),
               waits=[(s_const, consts_v)], inc=cinc(s_gpe))
        for h in range(HEADS):
            for tc in range(4):
                col = (h * 4 + tc) * VST + 128
                rec.op("gpsimd", lambda E, c=col: E.memset(vtok2[0][:, c:c + 1], 1.0),
                       inc=cinc(s_gpe))
        v_ones = {0: s_gpe.n, 1: 0}

        scw1_g = scw1.rearrange("(g p k) c -> g p k c", p=128, k=DC)
        scw2_m = scw2.rearrange("(m p k) c -> m p k c", p=128, k=HC)
        scq_o = scqkv.rearrange("(o p k) c -> o p k c", p=128, k=DC)

        def pfs(sl):
            return pf32[:, sl * SLW:(sl + 1) * SLW] if sl < 2 else pf32c[:, 0:SLW]

        def pbs(sl):
            return pbf[:, sl * SLW:(sl + 1) * SLW]

        NSL = 3
        slot_gate = [projb_gate, None, None]
        wrf = [0] * NSL

        def conv_step(it, load_src, consume_fn):
            sl = it % NSL
            w = [slot_gate[sl]] if slot_gate[sl] else []
            v_l = sdma(s_wload[sl], pfs(sl), load_src, waits=w)
            slot_gate[sl] = consume_fn(sl, [(s_wload[sl], v_l)])

        def sign_and_write(write_dst_fn):
            def consume(sl, waits):
                if wrf[sl]:
                    waits = waits + [(s_scwr[sl], wrf[sl])]
                rec.op("scalar",
                       lambda E, sl=sl: E.sign(pbs(sl), pfs(sl)),
                       waits=waits, inc=cinc(s_act))
                sv = s_act.n
                rec.op("scalarq", write_dst_fn(sl), waits=[(s_act, sv)],
                       inc=(s_scwr[sl], 16))
                wrf[sl] = s_scwr[sl].inc(16)
                return (s_act, sv)
            return consume

        w1grp_wait = {}

        a1col_v = [0] * HC
        a2col_v = [0] * DC
        deferred = {}
        ASL = 2
        aslot_gate = [None] * ASL
        alpha_loads = {}

        def alpha_issue(it, load_src):
            sl = it % ASL
            w = [aslot_gate[sl]] if aslot_gate[sl] else []
            alpha_loads[it] = gdma(s_al[sl], pa[:, sl * SLW:(sl + 1) * SLW],
                                   load_src, waits=w)

        def emit_alpha1():
            srcs = [w1n[hcn * 128:(hcn + 1) * 128, :] for hcn in range(HC)]
            for it in range(ASL):
                alpha_issue(it, srcs[it])
            for hcn in range(HC):
                sl = hcn % ASL
                rec.op("vector",
                       lambda E, sl=sl, c=hcn: E.tensor_reduce(
                           alpha1[:, c:c + 1], pa[:, sl * SLW:(sl + 1) * SLW],
                           axis=AXX, op=ALU.add, apply_absolute_value=True),
                       waits=[(s_al[sl], alpha_loads[hcn])], inc=cinc(s_dve))
                rec.op("vector",
                       lambda E, c=hcn: E.tensor_scalar_mul(
                           alpha1[:, c:c + 1], alpha1[:, c:c + 1], 1.0 / D),
                       inc=cinc(s_dve))
                a1col_v[hcn] = s_dve.n
                aslot_gate[sl] = (s_dve, s_dve.n)
                if hcn + ASL < HC:
                    alpha_issue(hcn + ASL, srcs[hcn + ASL])

        def emit_deferred_rest():
            srcs = [w2n[dcn * 128:(dcn + 1) * 128, p * 1024:(p + 1) * 1024]
                    for dcn in range(DC) for p in range(4)]
            for it in range(ASL):
                alpha_issue(32 + it, srcs[it])
            for j in range(32):
                dcn, piece = j // 4, j % 4
                sl = (32 + j) % ASL
                rec.op("vector",
                       lambda E, sl=sl, p=piece: E.tensor_reduce(
                           patm[:, p:p + 1], pa[:, sl * SLW:(sl + 1) * SLW],
                           axis=AXX, op=ALU.add, apply_absolute_value=True),
                       waits=[(s_al[sl], alpha_loads[32 + j])], inc=cinc(s_dve))
                aslot_gate[sl] = (s_dve, s_dve.n)
                if j + ASL < 32:
                    alpha_issue(32 + j + ASL, srcs[j + ASL])
                if piece == 3:
                    rec.op("vector",
                           lambda E, c=dcn: E.tensor_reduce(
                               alpha2[:, c:c + 1], patm[:], axis=AXX, op=ALU.add),
                           inc=cinc(s_dve))
                    rec.op("vector",
                           lambda E, c=dcn: E.tensor_scalar_mul(
                               alpha2[:, c:c + 1], alpha2[:, c:c + 1], 0.5 / H),
                           inc=cinc(s_dve))
                    a2col_v[dcn] = s_dve.n
            deferred['alpha_done'] = s_dve.n

            wrq = [0] * NSL
            def cast_and_write(write_dst_fn):
                def consume(sl, waits):
                    if wrf[sl]:
                        waits = waits + [(s_scwr[sl], wrf[sl])]
                    if wrq[sl]:
                        waits = waits + [(s_scq[sl], wrq[sl])]
                    rec.op("gpsimd",
                           lambda E, sl=sl: E.tensor_copy(pbs(sl), pfs(sl)),
                           waits=waits, inc=cinc(s_gpe))
                    cv = s_gpe.n
                    rec.op("gpsimdq", write_dst_fn(sl), waits=[(s_gpe, cv)],
                           inc=(s_scq[sl], 16))
                    wrq[sl] = s_scq[sl].inc(16)
                    return (s_gpe, cv)
                return consume

            for kc in range(DC):
                for hh3 in range(3):
                    def _wr(sl, kc=kc, hh3=hh3):
                        return lambda E, sl=sl: E.dma_start(
                            out=scq_o[hh3 * 8:(hh3 + 1) * 8, :, kc, :].rearrange(
                                "o p c -> p o c"),
                            in_=pbs(sl).rearrange(
                                "p (o c) -> p o c", c=128))
                    conv_step(64 + kc * 3 + hh3,
                              wqkv[kc * 128:(kc + 1) * 128,
                                   hh3 * 1024:(hh3 + 1) * 1024],
                              cast_and_write(_wr))
            qkv_scr_wait = [(s, s.n) for s in s_scq]

            for kc in range(DC):
                def _cast(sl, waits, c=kc):
                    rec.op("gpsimd",
                           lambda E, sl=sl, c=c: E.tensor_copy(
                               woutb[:, c * 1024:(c + 1) * 1024], pfs(sl)),
                           waits=waits, inc=cinc(s_gpe))
                    return (s_gpe, s_gpe.n)
                conv_step(88 + kc, wout[kc * 128:(kc + 1) * 128, :], _cast)
            deferred['wout_done'] = s_gpe.n
            deferred['qkv_scr'] = qkv_scr_wait


        def ln_emit(c_, buf, off, W, buf_ready=None):
            bsum, bsq = PB[c_["sbk"][0]], PB[c_["sbk"][1]]
            bcA, bcB = bsum, bsq
            r1row, r2row = c_["r1"], c_["r2"]
            xcrot, sqrot, scr, xn = c_["xc"], c_["sq"], c_["scr"], c_["xn"]
            xc_free, sq_free = c_["xc_free"], c_["sq_free"]
            for c in range(DC):
                sl = c % 2
                w = list(xc_free[sl]) if xc_free[sl] else []
                if c == 0 and buf_ready is not None:
                    w.append(buf_ready)
                rec.op("vector",
                       lambda E, c=c, sl=sl: E.tensor_copy(
                           xcrot[:, sl * MAXE: sl * MAXE + W],
                           buf[:, c * XE + off: c * XE + off + W]),
                       waits=w, inc=cinc(s_dve))
                cast_v = s_dve.n
                w = [(s_dve, cast_v)]
                if sq_free[sl]:
                    w.append((s_pe, sq_free[sl]))
                rec.op("scalar",
                       lambda E, sl=sl: E.square(sqrot[:, sl * MAXE: sl * MAXE + W],
                                                 xcrot[:, sl * MAXE: sl * MAXE + W]),
                       waits=w, inc=cinc(s_act))
                sq_v = s_act.n
                t_mm(bsum[0:1, 0:W], onb[:, 0:1], xcrot[:, sl * MAXE: sl * MAXE + W],
                     start=(c == 0), stop=(c == DC - 1),
                     waits=[(s_dve, cast_v)] + (bank_wait(c_["sbk"][0]) if c == 0 else []),
                     inc=cinc(s_pe))
                xc_free[sl] = [(s_pe, s_pe.n), (s_act, sq_v)]
                t_mm(bsq[0:1, 0:W], onb[:, 0:1], sqrot[:, sl * MAXE: sl * MAXE + W],
                     start=(c == 0), stop=(c == DC - 1),
                     waits=[(s_act, sq_v)] + (bank_wait(c_["sbk"][1]) if c == 0 else []),
                     inc=cinc(s_pe))
                sq_free[sl] = s_pe.n
            stats_mm = s_pe.n
            rec.op("scalar",
                   lambda E: E.activation(r1row[:, 0:W], bsum[0:1, 0:W],
                                          AF.Square, scale=ccs[0:1, 2:3]),
                   waits=[(s_pe, stats_mm)], inc=cinc(s_act))
            m2_v = s_act.n
            rec.op("vector",
                   lambda E: E.scalar_tensor_tensor(r2row[:, 0:W], bsq[0:1, 0:W],
                                                    1.0 / D, r1row[:, 0:W],
                                                    ALU.mult, ALU.subtract),
                   waits=[(s_act, m2_v)], inc=cinc(s_dve))
            var_v = s_dve.n
            rec.op("scalar",
                   lambda E: E.activation(r1row[:, 0:W], r2row[:, 0:W],
                                          AF.Ln, bias=ccs[0:1, 0:1]),
                   waits=[(s_dve, var_v)], inc=cinc(s_act))
            rec.op("scalar",
                   lambda E: E.activation(r2row[:, 0:W], r1row[:, 0:W],
                                          AF.Exp, scale=ccs[0:1, 1:2]),
                   inc=cinc(s_act))
            rstd_v = s_act.n
            rec.op("vector",
                   lambda E: E.scalar_tensor_tensor(r1row[:, 0:W], bsum[0:1, 0:W],
                                                    1.0 / D, r2row[:, 0:W],
                                                    ALU.mult, ALU.mult),
                   waits=[(s_act, rstd_v)], inc=cinc(s_dve))
            mr_v = s_dve.n
            t_mm(bcA[:, 0:W], on1[:], r2row[:, 0:W], True, True,
                 waits=[(s_dve, mr_v)], inc=cinc(s_pe))
            t_mm(bcB[:, 0:W], on1[:], r1row[:, 0:W], True, True,
                 waits=[(s_dve, mr_v)], inc=cinc(s_pe))
            bc_v = s_pe.n
            xn_vc = [0] * DC
            for c in range(DC):
                sl = c % 2
                rec.op("vector",
                       lambda E, c=c, sl=sl: E.tensor_tensor(
                           scr[sl][:, 0:W], buf[:, c * XE + off: c * XE + off + W],
                           bcA[:, 0:W], ALU.mult),
                       waits=[(s_pe, bc_v)] if c == 0 else (),
                       inc=cinc(s_dve))
                w2_ = [(s_pe, c_["xn_rd"])] if (c == 0 and c_["xn_rd"]) else []
                rec.op("vector",
                       lambda E, c=c, sl=sl: E.tensor_tensor(
                           xn[:, c * MAXE: c * MAXE + W], scr[sl][:, 0:W],
                           bcB[:, 0:W], ALU.subtract),
                       waits=w2_, inc=cinc(s_dve))
                xn_vc[c] = s_dve.n
            xn_v = s_dve.n
            bank_free[c_["sbk"][0]] = [(s_dve, xn_v)]
            bank_free[c_["sbk"][1]] = [(s_dve, xn_v)]
            if False:
                yield
            return xn_vc

        w1f = {"v": {0: 0, 1: 0}, "freed": {0: 0, 1: 0}}
        w2f = {"v": {0: 0, 1: 0}, "freed": {0: 0, 1: 0}}
        qkf = {"v": {0: 0, 1: 0}, "freed": {0: 0, 1: 0}}
        last_ffn_resid = [0]

        ffn2_h_emitted = {}

        def ffn_emit(fbh, fbo, xn_c, scr, buf, off, W, first_tile, xn_vc,
                     mid_hook=None, h_flag=None):
            xn = xn_c["xn"]
            evac_last = 0
            for hc in range(HC):
                gidx, slot = hc // 4, (hc // 4) % 2
                if first_tile and hc % 8 == 0:
                    piece = hc // 8
                    for kc in range(DC):
                        def _wr(sl, piece=piece, kc=kc):
                            return lambda E, sl=sl: E.dma_start(
                                out=scw1_g[piece * 2: piece * 2 + 2, :, kc, :].rearrange(
                                    "g p c -> p g c"),
                                in_=pbs(sl).rearrange(
                                    "p (g c) -> p g c", c=512))
                        conv_step(piece * DC + kc,
                                  w1t[kc * 128:(kc + 1) * 128,
                                      piece * 1024:(piece + 1) * 1024],
                                  sign_and_write(_wr))
                    snap = [(s, s.n) for s in s_scwr]
                    w1grp_wait[piece * 2] = snap
                    w1grp_wait[piece * 2 + 1] = snap
                    for kc2 in range(piece * 8, piece * 8 + 8):
                        def _wr2(sl, kc2=kc2):
                            return lambda E, sl=sl: E.dma_start(
                                out=scw2_m[:, :, kc2, :].rearrange("m p c -> p m c"),
                                in_=pbs(sl).rearrange(
                                    "p (m c) -> p m c", c=128))
                        conv_step(32 + kc2, w2t[kc2 * 128:(kc2 + 1) * 128, :],
                                  sign_and_write(_wr2))
                    if piece == 3:
                        deferred['w2_scr'] = [(s, s.n) for s in s_scwr]
                if hc % 4 == 0:
                    w = list(w1grp_wait[gidx]) if first_tile else []
                    if w1f["freed"][slot]:
                        w.append((s_pe, w1f["freed"][slot]))
                    w1f["v"][slot] = sdma(
                        s_fw1[slot],
                        w1s[:, slot * 4096:(slot + 1) * 4096].rearrange(
                            "p (k c) -> p k c", c=512),
                        scw1_g[gidx], waits=w)
                bid = fbh[hc % 2]
                for kc in range(DC):
                    waits = ([(s_fw1[slot], w1f["v"][slot])] + bank_wait(bid)) if kc == 0 else []
                    if hc == 0:
                        waits = waits + [(s_dve, xn_vc[kc])]
                    t_mm(PB[bid][:, 0:W],
                         w1s[:, slot * 4096 + kc * 512 + (hc % 4) * 128:
                             slot * 4096 + kc * 512 + (hc % 4) * 128 + 128],
                         xn[:, kc * MAXE: kc * MAXE + W],
                         start=(kc == 0), stop=(kc == DC - 1),
                         waits=waits, inc=cinc(s_pe) if kc == DC - 1 else None)
                mm_v = s_pe.n
                if hc % 4 == 3:
                    w1f["freed"][slot] = mm_v
                wg = [(s_dve, a1col_v[hc]), (s_pe, mm_v)]
                rec.op("scalar",
                       lambda E, hc=hc, bid=bid: E.activation(
                           g[:, hc * MAXE: hc * MAXE + W], PB[bid][:, 0:W], AF.Gelu,
                           scale=alpha1[:, hc:hc + 1], bias=b1s[:, hc:hc + 1]),
                       waits=wg,
                       inc=cinc(s_act))
                evac_last = s_act.n
                bank_free[bid] = [(s_act, evac_last)]
                yield
            xn_c["xn_rd"] = s_pe.n
            if h_flag is not None:
                ffn2_h_emitted[h_flag] = True
            if mid_hook is not None:
                mid_hook()
            resid_free = {0: 0, 1: 0}
            for mc in range(DC):
                slot = mc % 2
                w = list(deferred['w2_scr']) if first_tile else []
                if w2f["freed"][slot]:
                    w.append((s_pe, w2f["freed"][slot]))
                w2f["v"][slot] = sdma(
                    s_fw2[slot],
                    w2s[:, slot * 4096:(slot + 1) * 4096].rearrange(
                        "p (k c) -> p k c", c=128),
                    scw2_m[mc], waits=w)
                bid = fbo[mc % 2]
                bw = bank_wait(bid)
                for kc2 in range(HC):
                    waits = ([(s_fw2[slot], w2f["v"][slot]), (s_act, evac_last)] + bw) \
                        if kc2 == 0 else []
                    t_mm(PB[bid][:, 0:W],
                         w2s[:, slot * 4096 + kc2 * 128: slot * 4096 + kc2 * 128 + 128],
                         g[:, kc2 * MAXE: kc2 * MAXE + W],
                         start=(kc2 == 0), stop=(kc2 == HC - 1),
                         waits=waits, inc=cinc(s_pe) if kc2 == HC - 1 else None)
                mm_v = s_pe.n
                w2f["freed"][slot] = mm_v
                sl = mc % 2
                w = [(s_dve, a2col_v[mc]), (s_pe, mm_v)]
                if resid_free[sl]:
                    w.append((s_dve, resid_free[sl]))
                rec.op("scalar",
                       lambda E, bid=bid, sl=sl, mc=mc: E.activation(
                           scr[sl][:, 0:W], PB[bid][:, 0:W], AF.Identity,
                           scale=alpha2[:, mc:mc + 1], bias=b2s[:, mc:mc + 1]),
                       waits=w, inc=cinc(s_act))
                ev = s_act.n
                rec.op("vector",
                       lambda E, mc=mc, sl=sl: E.tensor_add(
                           buf[:, mc * XE + off: mc * XE + off + W],
                           buf[:, mc * XE + off: mc * XE + off + W],
                           scr[sl][:, 0:W]),
                       waits=[(s_act, ev)], inc=cinc(s_dve))
                resid_free[sl] = s_dve.n
                bank_free[bid] = [(s_act, ev)]
                yield
            last_ffn_resid[0] = s_dve.n

        phi_done = {0: {}, 1: {}}
        v_done = {0: {}, 1: {}}
        att_done = {}
        qread = {0: 0, 1: 0}
        attnT_rd = [0]

        def qkv_emit(E, first_tile, xn_vc, parity, init_ones=False):
            qphi, kphi, vtok = qphi2[parity], kphi2[parity], vtok2[parity]
            xn, scr = S1c["xn"], S1c["scr"]
            scrw = list(deferred['qkv_scr']) if first_tile else []
            if first_tile:
                scrw.append((s_gpe, deferred['wout_done']))
            qkrot_free = {0: 0, 1: 0}
            phi_free = {0: 0, 1: 0}
            scr_free = {0: 0, 1: 0}
            qk_ev = {}

            def qk_part(oc):
                slot = oc % 2
                w = list(scrw)
                if qkf["freed"][slot]:
                    w.append((s_pe, qkf["freed"][slot]))
                qkf["v"][slot] = sdma(
                    s_fqk[slot],
                    qks[:, slot * 1024:(slot + 1) * 1024].rearrange(
                        "p (k c) -> p k c", c=128),
                    scq_o[oc], waits=w)
                bid = 2 + slot
                bw = bank_wait(bid)
                for kc in range(DC):
                    waits = ([(s_fqk[slot], qkf["v"][slot])] + bw) if kc == 0 else []
                    if oc == 0:
                        waits = waits + [(s_dve, xn_vc[kc])]
                    t_mm(PB[bid][:, 0:E],
                         qks[:, slot * 1024 + kc * 128: slot * 1024 + kc * 128 + 128],
                         xn[:, kc * MAXE: kc * MAXE + E],
                         start=(kc == 0), stop=(kc == DC - 1),
                         waits=waits, inc=cinc(s_pe) if kc == DC - 1 else None)
                mm_v = s_pe.n
                qkf["freed"][slot] = mm_v
                w = [(s_pe, mm_v)]
                if qkrot_free[slot]:
                    w.append((s_pe, qkrot_free[slot]))
                rec.op("scalar",
                       lambda E_, sl=slot, bid=bid: E_.copy(
                           qkrot[:, sl * MAXE: sl * MAXE + E], PB[bid][:, 0:E]),
                       waits=w, inc=cinc(s_act))
                qk_ev[oc] = s_act.n
                bank_free[bid] = [(s_act, qk_ev[oc])]

            def phi_part(oc):
                slot = oc % 2
                pbid = 0 + slot
                head = oc % 8
                w = [(s_act, qk_ev[oc])] + bank_wait(pbid)
                if phi_free[slot]:
                    w.append((s_dve, phi_free[slot]))
                w.append((s_gpe, projb_v))
                t_mm(PB[pbid][0:FEAT, 0:E], projb[:],
                     qkrot[:, slot * MAXE: slot * MAXE + E],
                     True, True, waits=w, inc=cinc(s_pe))
                phi_v = s_pe.n
                qkrot_free[slot] = phi_v
                rows = (head % 2) * 64
                w = [(s_pe, phi_v)]
                if scr_free[slot]:
                    w.append((s_dve, scr_free[slot]))
                rec.op("scalar",
                       lambda E_, pbid=pbid, rows=rows: E_.activation(
                           scr[0][rows:rows + 64, 0:E], PB[pbid][0:FEAT, 0:E], AF.Relu),
                       waits=w, inc=cinc(s_act))
                r_v = s_act.n
                rec.op("vector",
                       lambda E_, pbid=pbid, rows=rows: E_.tensor_sub(
                           scr[1][rows:rows + 64, 0:E], PB[pbid][0:FEAT, 0:E],
                           scr[0][rows:rows + 64, 0:E]),
                       waits=[(s_act, r_v)], inc=cinc(s_dve))
                m_v = s_dve.n
                phi_free[slot] = m_v
                bank_free[pbid] = [(s_dve, m_v)]
                rec.op("scalar",
                       lambda E_, rows=rows: E_.activation(
                           scr[2][rows:rows + 64, 0:E], scr[1][rows:rows + 64, 0:E],
                           AF.Exp),
                       waits=[(s_dve, m_v)], inc=cinc(s_act))
                e_v = s_act.n
                dst = qphi if oc < 8 else kphi
                pair = head // 2
                w = [(s_act, e_v)]
                if oc in (0, 8) and qread[parity]:
                    w.append((s_pe, qread[parity]))
                rec.op("vector",
                       lambda E_, dst=dst, pair=pair, rows=rows: E_.tensor_add(
                           dst[rows:rows + 64, pair * MAXE: pair * MAXE + E],
                           scr[2][rows:rows + 64, 0:E], scr[0][rows:rows + 64, 0:E]),
                       waits=w, inc=cinc(s_dve))
                scr_free[slot] = s_dve.n
                phi_done[parity][oc] = s_dve.n

            for step in range(17):
                if step < 16:
                    qk_part(step)
                if step >= 1:
                    phi_part(step - 1)
                yield
            S1c["xn_rd"] = s_pe.n
            if init_ones:
                for h in range(HEADS):
                    for tc in range(4):
                        col = (h * 4 + tc) * VST + 128
                        w = [(s_dve, deferred['alpha_done'])] if (h == 0 and tc == 0) else []
                        rec.op("gpsimd",
                               lambda E_, c=col: E_.memset(vtok[:, c:c + 1], 1.0),
                               waits=w, inc=cinc(s_gpe))
                v_ones[parity] = s_gpe.n
            nch = (E + 127) // 128
            vr_free = {0: 0, 1: 0}
            first_copy = True
            for h2 in range(2):
                vmm_v = 0
                for kc in range(DC):
                    slot = kc % 2
                    w = list(scrw)
                    if vr_free[slot]:
                        w.append((s_pe, vr_free[slot]))
                    v_l = sdma(s_fvr[slot],
                               vrs[:, slot * 512:(slot + 1) * 512].rearrange(
                                   "p (o c) -> p o c", c=128),
                               scq_o[16 + h2 * 4: 16 + h2 * 4 + 4, :, kc, :].rearrange(
                                   "o p c -> p o c"),
                               waits=w)
                    for tc in range(nch):
                        rows = min(128, E - tc * 128)
                        bid = tc
                        waits = [(s_fvr[slot], v_l)] + (bank_wait(bid) if kc == 0 else [])
                        t_mm(PB[bid][0:rows, 0:512],
                             xn[:, kc * MAXE + tc * 128: kc * MAXE + tc * 128 + rows],
                             vrs[:, slot * 512:(slot + 1) * 512],
                             start=(kc == 0), stop=(kc == DC - 1),
                             waits=waits,
                             inc=cinc(s_pe) if tc == nch - 1 else None)
                    vr_free[slot] = s_pe.n
                    yield
                vmm_v = s_pe.n
                for tc in range(nch):
                    rows = min(128, E - tc * 128)
                    bid = tc
                    for hh in range(4):
                        head = h2 * 4 + hh
                        col = (head * 4 + tc) * VST
                        w = [(s_pe, vmm_v)]
                        if first_copy:
                            first_copy = False
                            if qread[parity]:
                                w.append((s_pe, qread[parity]))
                            if init_ones:
                                w.append((s_gpe, v_ones[parity]))
                        rec.op("scalar",
                               lambda E_, bid=bid, rows=rows, hh=hh, col=col, vtok=vtok:
                               E_.copy(vtok[0:rows, col:col + 128],
                                       PB[bid][0:rows, hh * 128: hh * 128 + 128]),
                               waits=w, inc=cinc(s_act))
                    bank_free[bid] = [(s_act, s_act.n)]
                v_done[parity][h2] = s_act.n
            S1c["xn_rd"] = s_pe.n

        attn_state = {"smA_rd": {0: 0, 1: 0}, "an_rd": {0: 0, 1: 0},
                      "bankA": {0: [], 1: []}, "bankO": {0: [], 1: []},
                      "pt": {0: 0, 1: 0}, "an_hist": {0: 0, 1: 0}}

        def attn_emit(T, E, first, last, parity):
            qphi, kphi, vtok = qphi2[parity], kphi2[parity], vtok2[parity]
            nqb = (T + 127) // 128
            N = nqb * HEADS
            smA_rd, an_rd = attn_state["smA_rd"], attn_state["an_rd"]
            bankA, bankO = attn_state["bankA"], attn_state["bankO"]
            pt_free = attn_state["pt"]
            st = [dict() for _ in range(N)]
            first_cp = [True]

            def geom(c):
                m, head = c // HEADS, c % HEADS
                qb = min(128, T - m * 128)
                return m, head, qb, min(128, E - m * 128), max(0, qb - 122)

            def stage(s, c):
                m, head, qb, arows, brows = geom(c)
                slot = c % 2
                bid, bidT = 4 + slot, 6 + slot
                pair, rows = head // 2, (head % 2) * 64
                S = st[c]
                if s == 0:
                    w = list(bankA[slot]) + bank_wait(bid)
                    if m == 0:
                        w += [(s_dve, phi_done[parity][head]),
                              (s_dve, phi_done[parity][8 + head])]
                    t_mm(PB[bid][0:arows, 0:qb],
                         kphi[rows:rows + 64,
                              pair * MAXE + m * 128: pair * MAXE + m * 128 + arows],
                         qphi[rows:rows + 64,
                              pair * MAXE + 3 + m * 128: pair * MAXE + 3 + m * 128 + qb],
                         True, True, waits=w,
                         inc=cinc(s_pe) if brows == 0 else None)
                    if brows > 0:
                        t_mm(PB[bid][0:brows, 128:128 + qb],
                             kphi[rows:rows + 64,
                                  pair * MAXE + (m + 1) * 128:
                                  pair * MAXE + (m + 1) * 128 + brows],
                             qphi[rows:rows + 64,
                                  pair * MAXE + 3 + m * 128:
                                  pair * MAXE + 3 + m * 128 + qb],
                             True, True, waits=(), inc=cinc(s_pe))
                    S["sc"] = s_pe.n
                elif s == 1:
                    mvar = 0 if (first and m == 0) else (2 if (last and m == nqb - 1) else 1)
                    w = [(s_pe, S["sc"])]
                    if smA_rd[slot]:
                        w.append((s_pe, smA_rd[slot]))
                    rec.op("vector",
                           lambda E_, bid=bid, slot=slot, arows=arows, qb=qb, mvar=mvar:
                           E_.tensor_tensor(
                               smA[0:arows, slot * 128: slot * 128 + qb],
                               PB[bid][0:arows, 0:qb],
                               mA[0:arows, mvar * 128: mvar * 128 + qb], ALU.mult),
                           waits=w,
                           inc=cinc(s_dve) if brows == 0 else None)
                    if brows > 0:
                        rec.op("vector",
                               lambda E_, bid=bid, slot=slot, brows=brows, qb=qb:
                               E_.tensor_tensor(
                                   smB[0:brows, slot * 128: slot * 128 + qb],
                                   PB[bid][0:brows, 128:128 + qb],
                                   mB[0:brows, 0:qb], ALU.mult),
                               inc=cinc(s_dve))
                    S["sm"] = s_dve.n
                    bankA[slot] = [(s_dve, S["sm"])]
                elif s == 2:
                    w = [(s_dve, S["sm"]), (s_gpe, v_ones[parity]),
                         (s_act, v_done[parity][head // 4])] + list(bankO[slot])
                    t_mm(PB[bid][0:qb, 256:385],
                         smA[0:arows, slot * 128: slot * 128 + qb],
                         vtok[0:arows, (head * 4 + m) * VST: (head * 4 + m) * VST + 129],
                         True, (brows == 0), waits=w,
                         inc=cinc(s_pe) if brows == 0 else None)
                    if brows > 0:
                        t_mm(PB[bid][0:qb, 256:385],
                             smB[0:brows, slot * 128: slot * 128 + qb],
                             vtok[0:brows,
                                  (head * 4 + m + 1) * VST: (head * 4 + m + 1) * VST + 129],
                             False, True, waits=(), inc=cinc(s_pe))
                    S["at"] = s_pe.n
                    smA_rd[slot] = S["at"]
                elif s == 3:
                    w = [(s_pe, S["at"])]
                    an_prev = st[c - 2]["an"] if c >= 2 else attn_state["an_hist"][slot]
                    if an_prev:
                        w.append((s_act, an_prev))
                    rec.op("vector",
                           lambda E_, bid=bid, qb=qb, slot=slot: E_.tensor_scalar(
                               zcol[0:qb, 2 * slot: 2 * slot + 1],
                               PB[bid][0:qb, 384:385], EPS_Z, None, ALU.add),
                           waits=w)
                    rec.op("vector",
                           lambda E_, qb=qb, slot=slot: E_.reciprocal(
                               zcol[0:qb, 2 * slot + 1: 2 * slot + 2],
                               zcol[0:qb, 2 * slot: 2 * slot + 1]),
                           inc=cinc(s_dve))
                    S["zr"] = s_dve.n
                elif s == 4:
                    w = [(s_dve, S["zr"])]
                    if an_rd[slot]:
                        w.append((s_pe, an_rd[slot]))
                    rec.op("scalar",
                           lambda E_, bid=bid, qb=qb, slot=slot: E_.activation(
                               attn_n[0:qb, slot * 128: slot * 128 + 128],
                               PB[bid][0:qb, 256:384], AF.Identity,
                               scale=zcol[0:qb, 2 * slot + 1: 2 * slot + 2]),
                           waits=w, inc=cinc(s_act))
                    S["an"] = s_act.n
                    bankO[slot] = [(s_act, S["an"])]
                elif s == 5:
                    w = [(s_act, S["an"])]
                    if pt_free[slot]:
                        w.append((s_act, pt_free[slot]))
                    rec.op("tensor",
                           lambda E_, bidT=bidT, slot=slot, qb=qb: E_.transpose(
                               PB[bidT][0:128, 0:qb],
                               attn_n[0:qb, slot * 128: slot * 128 + 128],
                               id32[0:qb, 0:qb]),
                           waits=w, inc=cinc(s_pe))
                    S["tr"] = s_pe.n
                    an_rd[slot] = S["tr"]
                else:
                    w = [(s_pe, S["tr"])]
                    if first_cp[0]:
                        first_cp[0] = False
                        if attnT_rd[0]:
                            w.append((s_pe, attnT_rd[0]))
                        else:
                            w += list(deferred['qkv_scr'])
                    rec.op("scalar",
                           lambda E_, bidT=bidT, head=head, m=m, qb=qb: E_.copy(
                               attnT[:, head * MAXE + m * 128: head * MAXE + m * 128 + qb],
                               PB[bidT][0:128, 0:qb]),
                           waits=w, inc=cinc(s_act))
                    pt_free[slot] = s_act.n
                    if m == nqb - 1:
                        att_done[head] = s_act.n

            GROUPS = [(0,), (1, 2, 3, 4), (5, 6)]
            for t in range(N + len(GROUPS) - 1):
                for gi in range(len(GROUPS) - 1, -1, -1):
                    c = t - gi
                    if 0 <= c < N:
                        for s in GROUPS[gi]:
                            stage(s, c)
                yield
            qread[parity] = s_pe.n
            bank_free[4] = bankA[0] + bankO[0]
            bank_free[5] = bankA[1] + bankO[1]
            bank_free[6] = [(s_act, pt_free[0])]
            bank_free[7] = [(s_act, pt_free[1])]
            attn_state["an_hist"][0] = st[N - 2]["an"]
            attn_state["an_hist"][1] = st[N - 1]["an"]

        def outproj_emit(buf, T):
            scr = S2c["scr"]
            resid_free = {0: 0, 1: 0}
            for mc in range(DC):
                bid = 4 + (mc % 2)
                bw = bank_wait(bid)
                for kc in range(DC):
                    waits = ([(s_gpe, deferred['wout_done'])] + bw) if kc == 0 else []
                    if mc == 0:
                        waits = waits + [(s_act, att_done[kc])]
                    t_mm(PB[bid][:, 0:T],
                         woutb[:, kc * 1024 + mc * 128: kc * 1024 + mc * 128 + 128],
                         attnT[:, kc * MAXE: kc * MAXE + T],
                         start=(kc == 0), stop=(kc == DC - 1),
                         waits=waits, inc=cinc(s_pe) if kc == DC - 1 else None)
                mm_v = s_pe.n
                sl = mc % 2
                w = [(s_pe, mm_v)]
                if resid_free[sl]:
                    w.append((s_dve, resid_free[sl]))
                rec.op("scalar",
                       lambda E_, bid=bid, sl=sl, mc=mc, T=T: E_.activation(
                           scr[sl][:, 0:T], PB[bid][:, 0:T], AF.Identity,
                           bias=bouts[:, mc:mc + 1]),
                       waits=w, inc=cinc(s_act))
                ev = s_act.n
                rec.op("vector",
                       lambda E_, mc=mc, sl=sl, T=T: E_.tensor_add(
                           buf[:, mc * XE + 3: mc * XE + 3 + T],
                           buf[:, mc * XE + 3: mc * XE + 3 + T], scr[sl][:, 0:T]),
                       waits=[(s_act, ev)], inc=cinc(s_dve))
                resid_free[sl] = s_dve.n
                bank_free[bid] = [(s_act, ev)]
                yield
            attnT_rd[0] = s_pe.n

        def emit_store(t, resid_v):
            for c in range(DC):
                gdma(s_out, yT[c * 128:(c + 1) * 128, offs[t]: offs[t] + TILES[t]],
                     xb[t % 2][:, c * XE + 3: c * XE + 3 + TILES[t]],
                     waits=[(s_dve, resid_v)] if c == 0 else ())
            out_v[t] = s_out.n

        xnb_vc = {}
        ffn2_resid = {}

        def s1_window(j):
            if j >= 1:
                t = j - 1
                yield from ffn_emit((2, 3), (2, 3), S2c, scrA, xb[t % 2], 3,
                                    TILES[t], False, xnb_vc[t], h_flag=t)
                ffn2_resid[t] = last_ffn_resid[0]
                emit_store(t, ffn2_resid[t])
            if j + 1 < NT:
                t2 = j + 1
                if t2 >= 2:
                    load_x(t2)
                yield
                E2 = TILES[t2] + 6
                xv = yield from ln_emit(S1c, xb[t2 % 2], 0, E2,
                                        buf_ready=xload_v[t2])
                yield from ffn_emit((2, 3), (2, 3), S1c, scrA, xb[t2 % 2], 0,
                                    E2, False, xv)
                xv = yield from ln_emit(S1c, xb[t2 % 2], 0, E2)
                yield from qkv_emit(E2, False, xv, parity=t2 % 2,
                                    init_ones=(t2 == 1))

        def s2_window(j):
            E = TILES[j] + 6
            yield from attn_emit(TILES[j], E, first=(j == 0), last=(j == NT - 1),
                                 parity=j % 2)
            yield from outproj_emit(xb[j % 2], TILES[j])
            while j >= 1 and not ffn2_h_emitted.get(j - 1):
                yield
            xnb_vc[j] = yield from ln_emit(S2c, xb[j % 2], 3, TILES[j])
            yield

        def drain(gen):
            while True:
                try:
                    next(gen)
                except StopIteration as e:
                    return e.value

        def interleave(gb, ga, head=6, ga_per_gb=3):
            for _ in range(head):
                if next(gb, StopIteration) is StopIteration:
                    gb = None
                    break
            alive = [x for x in (gb, ga) if x is not None]
            while alive:
                if gb in alive:
                    if next(gb, StopIteration) is StopIteration:
                        alive.remove(gb)
                if ga in alive:
                    for _ in range(ga_per_gb):
                        if next(ga, StopIteration) is StopIteration:
                            alive.remove(ga)
                            break

        for e in ("tensor", "scalar", "vector"):
            rec.op(e, None, waits=[xload_v[0], (s_const, consts_v)])
        E0 = TILES[0] + 6
        xv0 = drain(ln_emit(S1c, xb[0], 0, E0))
        emit_alpha1()
        drain(ffn_emit((2, 3), (2, 3), S1c, scrA, xb[0], 0, E0, True, xv0,
                       mid_hook=emit_deferred_rest))
        xv0 = drain(ln_emit(S1c, xb[0], 0, E0))
        drain(qkv_emit(E0, True, xv0, parity=0))

        for j in range(NT):
            interleave(s2_window(j), s1_window(j))

        t = NT - 1
        drain(ffn_emit((2, 3), (2, 3), S2c, scrA, xb[t % 2], 3,
                       TILES[t], False, xnb_vc[t]))
        emit_store(t, last_ffn_resid[0])

        rec.op("gpsimd", None, waits=[(s_out, out_v[NT - 1])])

        block = ctx.enter_context(nc.Block())

        def emit_for(name):
            extra = {"gpsimd": {"gpsimdq"}, "scalar": {"scalarq"}}
            names = {name} | extra.get(name, set())

            def emitter(engobj):
                for eng, fn, waits, incs in rec.ops:
                    if eng not in names:
                        continue
                    for s, v in waits:
                        engobj.wait_ge(s.h, v)
                    if fn is None:
                        if incs:
                            inst = engobj.nop()
                            for s, amt in incs:
                                inst.then_inc(s.h, amt)
                        continue
                    inst = fn(engobj)
                    for s, amt in incs:
                        inst.then_inc(s.h, amt)
            return emitter

        block.tensor(emit_for("tensor"))
        block.scalar(emit_for("scalar"))
        block.vector(emit_for("vector"))
        block.sync(emit_for("sync"))
        block.gpsimd(emit_for("gpsimd"))

    return nc


def _host_inputs(L0_attn_x, L0_attn_w_qkv, L0_attn_proj, L0_attn_w_out, L0_attn_b_out,
                 L0_ffn_w1, L0_ffn_b1, L0_ffn_w2, L0_ffn_b2):
    x = np.asarray(L0_attn_x, dtype=np.float32)
    wqkv = np.ascontiguousarray(np.asarray(L0_attn_w_qkv, np.float32))
    projm = np.ascontiguousarray(np.asarray(L0_attn_proj, np.float32))
    woutm = np.ascontiguousarray(np.asarray(L0_attn_w_out, np.float32))
    bout = np.asarray(L0_attn_b_out, np.float32)
    w1 = np.asarray(L0_ffn_w1, np.float32)
    b1 = np.asarray(L0_ffn_b1, np.float32)
    w2 = np.asarray(L0_ffn_w2, np.float32)
    b2 = np.asarray(L0_ffn_b2, np.float32)

    w1t = np.ascontiguousarray(w1.T)
    w2t = np.ascontiguousarray(w2.T)
    b1cm = np.ascontiguousarray(b1.reshape(HC, 128).T)
    b2cm = np.ascontiguousarray(b2.reshape(DC, 128).T)
    boutcm = np.ascontiguousarray(bout.reshape(DC, 128).T)
    identf = np.eye(128, dtype=np.float32)
    onesb = np.ones((128, 1), dtype=ml_dtypes.bfloat16)
    ones1 = np.ones((1, 128), dtype=np.float32)
    cconst = np.broadcast_to(np.array([EPS_LN, -0.5, 1.0 / D], np.float32),
                             (128, 3)).copy()

    j = np.arange(128)[:, None]
    q = np.arange(128)[None, :]
    bandA = ((j - q >= 0) & (j - q <= 6)).astype(np.float32)
    jb = np.arange(6)[:, None]
    bandB = (q >= 122 + jb).astype(np.float32)
    T_last = TILES[-1]
    cut_last = (T_last + 3) - 384

    in_maps = []
    for core in range(8):
        b = core // 4
        start = (core % 4) * NTOK
        lo, hi = start - 3, start + NTOK + 3
        xs = np.zeros((NTOK + 6, D), np.float32)
        clo, chi = max(lo, 0), min(hi, 16384)
        xs[clo - lo: chi - lo, :] = x[b, clo:chi, :]
        xTm = np.ascontiguousarray(xs.T)

        Afirst = bandA.copy()
        Alast = bandA.copy()
        if start == 0:
            Afirst[0:3, :] = 0.0
        if start + NTOK == 16384:
            Alast[cut_last:cut_last + 3, :] = 0.0
        maskA3m = np.ascontiguousarray(np.concatenate([Afirst, bandA, Alast], axis=1))

        in_maps.append({
            "xT": xTm, "wqkv": wqkv, "wout": woutm, "proj": projm,
            "w1t": w1t, "w1n": w1, "w2t": w2t, "w2n": w2,
            "b1c": b1cm, "b2c": b2cm, "boutc": boutcm,
            "maskA3": maskA3m, "maskB": np.ascontiguousarray(bandB),
            "identf": identf, "onesb": onesb, "ones1": ones1, "cconst": cconst,
        })
    return in_maps


def kernel(L0_attn_x, L0_attn_w_qkv, L0_attn_proj, L0_attn_w_out, L0_attn_b_out,
           L0_ffn_w1, L0_ffn_b1, L0_ffn_w2, L0_ffn_b2):
    in_maps = _host_inputs(L0_attn_x, L0_attn_w_qkv, L0_attn_proj, L0_attn_w_out,
                           L0_attn_b_out, L0_ffn_w1, L0_ffn_b1, L0_ffn_w2, L0_ffn_b2)
    if "nc" not in _CACHED:
        _CACHED["nc"] = build_nc()
    nc = _CACHED["nc"]
    res = run_bass_kernel_spmd(nc, in_maps, core_ids=list(range(8)))
    out = np.empty((2, 16384, D), np.float32)
    for core in range(8):
        b = core // 4
        start = (core % 4) * NTOK
        out[b, start:start + NTOK, :] = res.results[core]["yT"].T
    return out


# version 33
# speedup vs baseline: 1.1267x; 1.0207x over previous
import numpy as np
import ml_dtypes
from contextlib import ExitStack

import concourse.bass as bass
import concourse.mybir as mybir
from concourse.bass_utils import run_bass_kernel_spmd

F32 = mybir.dt.float32
BF16 = mybir.dt.bfloat16
AF = mybir.ActivationFunctionType
ALU = mybir.AluOpType
AXX = mybir.AxisListType.X

D = 1024
DC = 8
H = 4096
HC = 32
HEADS = 8
DH = 128
FEAT = 64
NTOK = 4096
TILES = [456] * 8 + [448]
NT = len(TILES)
MAXE = 462
XE = 464
VST = 130
assert sum(TILES) == NTOK
EPS_LN = 1e-5
EPS_Z = 1e-8

_CACHED = {}


class SemC:
    def __init__(self, name):
        self.name = name
        self.h = None
        self.n = 0

    def inc(self, amt=1):
        self.n += amt
        return self.n


class Rec:
    def __init__(self, eng_sems=None):
        self.ops = []
        self.eng_sems = eng_sems or {}

    def op(self, eng, fn=None, waits=(), inc=None):
        incs = ()
        if inc is not None:
            incs = tuple(inc) if isinstance(inc, list) else (inc,)
        waits = list(waits)
        own = self.eng_sems.get(eng)
        if own is not None and fn is not None:
            own_amt = sum(a for (s, a) in incs if s is own)
            if own_amt == 0:
                own.inc(1)
                incs = incs + ((own, 1),)
                own_amt = 1
            waits.append((own, own.n - own_amt))
        waits = tuple((s, v) for (s, v) in waits if v > 0)
        for s, v in waits:
            assert v <= s.n, f"wait-on-future: eng={eng} sem={s.name} v={v} planned={s.n}"
        self.ops.append((eng, fn, waits, incs))


def cinc(sem):
    sem.inc(1)
    return (sem, 1)


def build_nc():
    nc = bass.Bass("TRN2", target_bir_lowering=False, debug=False)

    xT = nc.dram_tensor("xT", [D, NTOK + 6], F32, kind="ExternalInput").ap()
    wqkv = nc.dram_tensor("wqkv", [D, 3 * D], F32, kind="ExternalInput").ap()
    wout = nc.dram_tensor("wout", [D, D], F32, kind="ExternalInput").ap()
    proj = nc.dram_tensor("proj", [DH, FEAT], F32, kind="ExternalInput").ap()
    w1t = nc.dram_tensor("w1t", [D, H], F32, kind="ExternalInput").ap()
    w1n = nc.dram_tensor("w1n", [H, D], F32, kind="ExternalInput").ap()
    w2t = nc.dram_tensor("w2t", [H, D], F32, kind="ExternalInput").ap()
    w2n = nc.dram_tensor("w2n", [D, H], F32, kind="ExternalInput").ap()
    b1c = nc.dram_tensor("b1c", [128, HC], F32, kind="ExternalInput").ap()
    b2c = nc.dram_tensor("b2c", [128, DC], F32, kind="ExternalInput").ap()
    boutc = nc.dram_tensor("boutc", [128, DC], F32, kind="ExternalInput").ap()
    maskA3 = nc.dram_tensor("maskA3", [128, 3 * 128], F32, kind="ExternalInput").ap()
    maskB = nc.dram_tensor("maskB", [6, 128], F32, kind="ExternalInput").ap()
    identf = nc.dram_tensor("identf", [128, 128], F32, kind="ExternalInput").ap()
    onesb = nc.dram_tensor("onesb", [128, 1], BF16, kind="ExternalInput").ap()
    ones1 = nc.dram_tensor("ones1", [1, 128], F32, kind="ExternalInput").ap()
    cconst = nc.dram_tensor("cconst", [128, 3], F32, kind="ExternalInput").ap()
    yT = nc.dram_tensor("yT", [D, NTOK], F32, kind="ExternalOutput").ap()

    scw1 = nc.dram_tensor("scw1", [8 * 128 * DC, 512], BF16, kind="Internal").ap()
    scw2 = nc.dram_tensor("scw2", [DC * 128 * HC, 128], BF16, kind="Internal").ap()
    scqkv = nc.dram_tensor("scqkv", [24 * 128 * DC, 128], BF16, kind="Internal").ap()

    s_pe = SemC("s_pe")
    s_act = SemC("s_act")
    s_dve = SemC("s_dve")
    s_gpe = SemC("s_gpe")
    s_const = SemC("s_const")
    s_wload = [SemC(f"s_wl{i}") for i in range(4)]
    s_scwr = [SemC(f"s_scwr{i}") for i in range(4)]
    s_al = [SemC(f"s_al{i}") for i in range(4)]
    s_scq = [SemC(f"s_scq{i}") for i in range(4)]
    s_fw1 = [SemC("s_fw1a"), SemC("s_fw1b")]
    s_fw2 = [SemC("s_fw2a"), SemC("s_fw2b")]
    s_fqk = [SemC("s_fqka"), SemC("s_fqkb")]
    s_fvr = [SemC("s_fvra"), SemC("s_fvrb")]
    s_x = [SemC("s_x0"), SemC("s_x1")]
    s_out = SemC("s_out")
    sems = ([s_pe, s_act, s_dve, s_gpe, s_const, s_out]
            + s_wload + s_scwr + s_al + s_scq
            + s_fw1 + s_fw2 + s_fqk + s_fvr + s_x)
    rec = Rec(eng_sems={"scalar": s_act, "vector": s_dve, "gpsimd": s_gpe})

    with ExitStack() as ctx:
        sb = lambda n, s, d: ctx.enter_context(nc.sbuf_tensor(n, s, d))
        psa = lambda n, s, d: ctx.enter_context(nc.psum_tensor(n, s, d))

        xb = [sb("xb0", [128, DC * XE], F32), sb("xb1", [128, DC * XE], F32)]
        xn_a = sb("xn_a", [128, DC * MAXE], BF16)
        xn_b = sb("xn_b", [128, DC * MAXE], BF16)
        xc_a = sb("xc_a", [128, 2 * MAXE], BF16)
        sq_a = sb("sq_a", [128, 2 * MAXE], BF16)
        xc_b = sb("xc_b", [128, 2 * MAXE], BF16)
        sq_b = sb("sq_b", [128, 2 * MAXE], BF16)
        scrA = [sb(f"scrA{i}", [128, MAXE], F32) for i in range(3)]
        scrB = [sb(f"scrB{i}", [128, MAXE], F32) for i in range(3)]
        rows_t = sb("rows_t", [1, 4 * MAXE], F32)
        g = sb("g", [128, HC * MAXE], BF16)
        qkrot = sb("qkrot", [128, 2 * MAXE], BF16)
        qphi2 = [sb("qphi0", [128, 4 * MAXE], BF16), sb("qphi1", [128, 4 * MAXE], BF16)]
        kphi2 = [sb("kphi0", [128, 4 * MAXE], BF16), sb("kphi1", [128, 4 * MAXE], BF16)]
        vtok2 = [sb("vtok0", [128, HEADS * 4 * VST], BF16),
                 sb("vtok1", [128, HEADS * 4 * VST], BF16)]
        smA = sb("smA", [128, 2 * 128], BF16)
        smB = sb("smB", [6, 2 * 128], BF16)
        attn_n = sb("attn_n", [128, 2 * 128], F32)
        attnT = sb("attnT", [128, HEADS * MAXE], BF16)
        zcol = sb("zcol", [128, 4], F32)
        w1s = sb("w1s", [128, 2 * 4096], BF16)
        w2s = sb("w2s", [128, 2 * 4096], BF16)
        qks = sb("qks", [128, 2 * 1024], BF16)
        vrs = sb("vrs", [128, 2 * 512], BF16)
        woutb = sb("woutb", [128, DC * 1024], BF16)
        projb = sb("projb", [128, FEAT], BF16)
        alpha1 = sb("alpha1", [128, HC], F32)
        alpha2 = sb("alpha2", [128, DC], F32)
        b1s = sb("b1s", [128, HC], F32)
        b2s = sb("b2s", [128, DC], F32)
        bouts = sb("bouts", [128, DC], F32)
        mA = sb("mA", [128, 3 * 128], F32)
        mB = sb("mB", [6, 128], F32)
        id32 = sb("id32", [128, 128], F32)
        onb = sb("onb", [128, 1], BF16)
        on1 = sb("on1", [1, 128], F32)
        ccs = sb("ccs", [128, 3], F32)
        pf32 = sb("pf32", [128, 2 * 1024], F32)
        patm = sb("patm", [128, 4], F32)

        def overlay(name, shape, dtype, host):
            hnd = nc.virtual_sbuf_tensor(name, shape, dtype)
            nc._tensor(hnd.name, hnd.shape, hnd.dtype, type="SB")
            mloc = nc.lookup_mloc(hnd)
            mloc.allocated = True
            mloc.addr = nc.lookup_mloc(host).addr
            return hnd

        pa = overlay("pa", [128, 2 * 1024], F32, vtok2[1])
        pbf = overlay("pbf", [128, 3 * 1024], BF16, attnT)
        pf32c = overlay("pf32c", [128, 1024], F32, xn_b)

        SLW = 1024
        PB = [psa(f"PB{i}", [128, 512], F32) for i in range(8)]

        r1a = rows_t[0:1, 0:MAXE]
        r2a = rows_t[0:1, MAXE:2 * MAXE]
        r1b = rows_t[0:1, 2 * MAXE:3 * MAXE]
        r2b = rows_t[0:1, 3 * MAXE:4 * MAXE]

        for s in sems:
            s.h = ctx.enter_context(nc.semaphore(s.name))

        bank_free = {}

        def bank_wait(bid):
            w = bank_free.get(bid)
            return list(w) if w else []

        def t_mm(out, lhsT, rhs, start, stop, waits=(), inc=None):
            rec.op("tensor",
                   lambda E, o=out, l=lhsT, r=rhs, st=start, sp=stop:
                   E.matmul(o, l, r, start=st, stop=sp),
                   waits=waits, inc=inc)

        def gdma(sem, dst, src, waits=()):
            v = sem.inc(16)
            rec.op("gpsimdq", lambda E, d=dst, s=src: E.dma_start(out=d, in_=s),
                   waits=waits, inc=(sem, 16))
            return v

        def sdma(sem, dst, src, waits=()):
            v = sem.inc(16)
            rec.op("sync", lambda E, d=dst, s=src: E.dma_start(out=d, in_=s),
                   waits=waits, inc=(sem, 16))
            return v

        S1c = {"xn": xn_a, "xc": xc_a, "sq": sq_a, "scr": scrA,
               "r1": r1a, "r2": r2a, "sbk": (0, 1),
               "xc_free": [None, None], "sq_free": [0, 0], "xn_rd": 0}
        S2c = {"xn": xn_b, "xc": xc_b, "sq": sq_b, "scr": scrB,
               "r1": r1b, "r2": r2b, "sbk": (4, 5),
               "xc_free": [None, None], "sq_free": [0, 0], "xn_rd": 0}

        offs = [sum(TILES[:i]) for i in range(NT)]
        xload_v = {}
        out_v = {}

        def load_x(i):
            buf = xb[i % 2]
            E = TILES[i] + 6
            waits = [] if i < 2 else [(s_out, out_v[i - 2])]
            sem = s_x[i % 2]
            for c in range(DC):
                gdma(sem, buf[:, c * XE: c * XE + E],
                     xT[c * 128:(c + 1) * 128, offs[i]: offs[i] + E],
                     waits=waits if c == 0 else ())
            xload_v[i] = (sem, sem.n)

        load_x(0)
        load_x(1)

        gdma(s_const, mA[:], maskA3)
        gdma(s_const, mB[:], maskB)
        gdma(s_const, id32[:], identf)
        gdma(s_const, onb[:], onesb)
        gdma(s_const, on1[:], ones1)
        gdma(s_const, ccs[:], cconst)
        gdma(s_const, b1s[:], b1c)
        gdma(s_const, b2s[:], b2c)
        gdma(s_const, bouts[:], boutc)
        gdma(s_const, pf32[:, 0:FEAT], proj)
        consts_v = s_const.n
        rec.op("gpsimd", lambda E: E.tensor_copy(projb[:], pf32[:, 0:FEAT]),
               waits=[(s_const, consts_v)], inc=cinc(s_gpe))
        projb_v = s_gpe.n
        projb_gate = (s_gpe, projb_v)
        rec.op("gpsimd", lambda E: E.tensor_scalar_mul(b2s[:], b2s[:], 0.5),
               waits=[(s_const, consts_v)], inc=cinc(s_gpe))
        for h in range(HEADS):
            for tc in range(4):
                col = (h * 4 + tc) * VST + 128
                rec.op("gpsimd", lambda E, c=col: E.memset(vtok2[0][:, c:c + 1], 1.0),
                       inc=cinc(s_gpe))
        v_ones = {0: s_gpe.n, 1: 0}

        scw1_g = scw1.rearrange("(g p k) c -> g p k c", p=128, k=DC)
        scw2_m = scw2.rearrange("(m p k) c -> m p k c", p=128, k=HC)
        scq_o = scqkv.rearrange("(o p k) c -> o p k c", p=128, k=DC)

        def pfs(sl):
            return pf32[:, sl * SLW:(sl + 1) * SLW] if sl < 2 else pf32c[:, 0:SLW]

        def pbs(sl):
            return pbf[:, sl * SLW:(sl + 1) * SLW]

        NSL = 3
        slot_gate = [projb_gate, None, None]
        wrf = [0] * NSL

        def conv_step(it, load_src, consume_fn):
            sl = it % NSL
            w = [slot_gate[sl]] if slot_gate[sl] else []
            v_l = sdma(s_wload[sl], pfs(sl), load_src, waits=w)
            slot_gate[sl] = consume_fn(sl, [(s_wload[sl], v_l)])

        def sign_and_write(write_dst_fn):
            def consume(sl, waits):
                if wrf[sl]:
                    waits = waits + [(s_scwr[sl], wrf[sl])]
                rec.op("scalar",
                       lambda E, sl=sl: E.sign(pbs(sl), pfs(sl)),
                       waits=waits, inc=cinc(s_act))
                sv = s_act.n
                rec.op("scalarq", write_dst_fn(sl), waits=[(s_act, sv)],
                       inc=(s_scwr[sl], 16))
                wrf[sl] = s_scwr[sl].inc(16)
                return (s_act, sv)
            return consume

        w1grp_wait = {}

        a1col_v = [0] * HC
        a2col_v = [0] * DC
        deferred = {}
        ASL = 2
        aslot_gate = [None] * ASL
        alpha_loads = {}

        def alpha_issue(it, load_src):
            sl = it % ASL
            w = [aslot_gate[sl]] if aslot_gate[sl] else []
            alpha_loads[it] = gdma(s_al[sl], pa[:, sl * SLW:(sl + 1) * SLW],
                                   load_src, waits=w)

        def emit_alpha1():
            srcs = [w1n[hcn * 128:(hcn + 1) * 128, :] for hcn in range(HC)]
            for it in range(ASL):
                alpha_issue(it, srcs[it])
            for hcn in range(HC):
                sl = hcn % ASL
                rec.op("vector",
                       lambda E, sl=sl, c=hcn: E.tensor_reduce(
                           alpha1[:, c:c + 1], pa[:, sl * SLW:(sl + 1) * SLW],
                           axis=AXX, op=ALU.add, apply_absolute_value=True),
                       waits=[(s_al[sl], alpha_loads[hcn])], inc=cinc(s_dve))
                rec.op("vector",
                       lambda E, c=hcn: E.tensor_scalar_mul(
                           alpha1[:, c:c + 1], alpha1[:, c:c + 1], 1.0 / D),
                       inc=cinc(s_dve))
                a1col_v[hcn] = s_dve.n
                aslot_gate[sl] = (s_dve, s_dve.n)
                if hcn + ASL < HC:
                    alpha_issue(hcn + ASL, srcs[hcn + ASL])

        def emit_deferred_rest():
            srcs = [w2n[dcn * 128:(dcn + 1) * 128, p * 1024:(p + 1) * 1024]
                    for dcn in range(DC) for p in range(4)]
            for it in range(ASL):
                alpha_issue(32 + it, srcs[it])
            for j in range(32):
                dcn, piece = j // 4, j % 4
                sl = (32 + j) % ASL
                rec.op("vector",
                       lambda E, sl=sl, p=piece: E.tensor_reduce(
                           patm[:, p:p + 1], pa[:, sl * SLW:(sl + 1) * SLW],
                           axis=AXX, op=ALU.add, apply_absolute_value=True),
                       waits=[(s_al[sl], alpha_loads[32 + j])], inc=cinc(s_dve))
                aslot_gate[sl] = (s_dve, s_dve.n)
                if j + ASL < 32:
                    alpha_issue(32 + j + ASL, srcs[j + ASL])
                if piece == 3:
                    rec.op("vector",
                           lambda E, c=dcn: E.tensor_reduce(
                               alpha2[:, c:c + 1], patm[:], axis=AXX, op=ALU.add),
                           inc=cinc(s_dve))
                    rec.op("vector",
                           lambda E, c=dcn: E.tensor_scalar_mul(
                               alpha2[:, c:c + 1], alpha2[:, c:c + 1], 0.5 / H),
                           inc=cinc(s_dve))
                    a2col_v[dcn] = s_dve.n
            deferred['alpha_done'] = s_dve.n

            wrq = [0] * NSL
            def cast_and_write(write_dst_fn):
                def consume(sl, waits):
                    if wrf[sl]:
                        waits = waits + [(s_scwr[sl], wrf[sl])]
                    if wrq[sl]:
                        waits = waits + [(s_scq[sl], wrq[sl])]
                    rec.op("gpsimd",
                           lambda E, sl=sl: E.tensor_copy(pbs(sl), pfs(sl)),
                           waits=waits, inc=cinc(s_gpe))
                    cv = s_gpe.n
                    rec.op("gpsimdq", write_dst_fn(sl), waits=[(s_gpe, cv)],
                           inc=(s_scq[sl], 16))
                    wrq[sl] = s_scq[sl].inc(16)
                    return (s_gpe, cv)
                return consume

            for kc in range(DC):
                for hh3 in range(3):
                    def _wr(sl, kc=kc, hh3=hh3):
                        return lambda E, sl=sl: E.dma_start(
                            out=scq_o[hh3 * 8:(hh3 + 1) * 8, :, kc, :].rearrange(
                                "o p c -> p o c"),
                            in_=pbs(sl).rearrange(
                                "p (o c) -> p o c", c=128))
                    conv_step(64 + kc * 3 + hh3,
                              wqkv[kc * 128:(kc + 1) * 128,
                                   hh3 * 1024:(hh3 + 1) * 1024],
                              cast_and_write(_wr))
            qkv_scr_wait = [(s, s.n) for s in s_scq]

            for kc in range(DC):
                def _cast(sl, waits, c=kc):
                    rec.op("gpsimd",
                           lambda E, sl=sl, c=c: E.tensor_copy(
                               woutb[:, c * 1024:(c + 1) * 1024], pfs(sl)),
                           waits=waits, inc=cinc(s_gpe))
                    return (s_gpe, s_gpe.n)
                conv_step(88 + kc, wout[kc * 128:(kc + 1) * 128, :], _cast)
            deferred['wout_done'] = s_gpe.n
            deferred['qkv_scr'] = qkv_scr_wait


        def ln_emit(c_, buf, off, W, buf_ready=None):
            bsum, bsq = PB[c_["sbk"][0]], PB[c_["sbk"][1]]
            bcA, bcB = bsum, bsq
            r1row, r2row = c_["r1"], c_["r2"]
            xcrot, sqrot, scr, xn = c_["xc"], c_["sq"], c_["scr"], c_["xn"]
            xc_free, sq_free = c_["xc_free"], c_["sq_free"]
            for c in range(DC):
                sl = c % 2
                w = list(xc_free[sl]) if xc_free[sl] else []
                if c == 0 and buf_ready is not None:
                    w.append(buf_ready)
                rec.op("vector",
                       lambda E, c=c, sl=sl: E.tensor_copy(
                           xcrot[:, sl * MAXE: sl * MAXE + W],
                           buf[:, c * XE + off: c * XE + off + W]),
                       waits=w, inc=cinc(s_dve))
                cast_v = s_dve.n
                w = [(s_dve, cast_v)]
                if sq_free[sl]:
                    w.append((s_pe, sq_free[sl]))
                rec.op("scalar",
                       lambda E, sl=sl: E.square(sqrot[:, sl * MAXE: sl * MAXE + W],
                                                 xcrot[:, sl * MAXE: sl * MAXE + W]),
                       waits=w, inc=cinc(s_act))
                sq_v = s_act.n
                t_mm(bsum[0:1, 0:W], onb[:, 0:1], xcrot[:, sl * MAXE: sl * MAXE + W],
                     start=(c == 0), stop=(c == DC - 1),
                     waits=[(s_dve, cast_v)] + (bank_wait(c_["sbk"][0]) if c == 0 else []),
                     inc=cinc(s_pe))
                xc_free[sl] = [(s_pe, s_pe.n), (s_act, sq_v)]
                t_mm(bsq[0:1, 0:W], onb[:, 0:1], sqrot[:, sl * MAXE: sl * MAXE + W],
                     start=(c == 0), stop=(c == DC - 1),
                     waits=[(s_act, sq_v)] + (bank_wait(c_["sbk"][1]) if c == 0 else []),
                     inc=cinc(s_pe))
                sq_free[sl] = s_pe.n
            stats_mm = s_pe.n
            rec.op("scalar",
                   lambda E: E.activation(r1row[:, 0:W], bsum[0:1, 0:W],
                                          AF.Square, scale=ccs[0:1, 2:3]),
                   waits=[(s_pe, stats_mm)], inc=cinc(s_act))
            m2_v = s_act.n
            rec.op("vector",
                   lambda E: E.scalar_tensor_tensor(r2row[:, 0:W], bsq[0:1, 0:W],
                                                    1.0 / D, r1row[:, 0:W],
                                                    ALU.mult, ALU.subtract),
                   waits=[(s_act, m2_v)], inc=cinc(s_dve))
            var_v = s_dve.n
            rec.op("scalar",
                   lambda E: E.activation(r1row[:, 0:W], r2row[:, 0:W],
                                          AF.Ln, bias=ccs[0:1, 0:1]),
                   waits=[(s_dve, var_v)], inc=cinc(s_act))
            rec.op("scalar",
                   lambda E: E.activation(r2row[:, 0:W], r1row[:, 0:W],
                                          AF.Exp, scale=ccs[0:1, 1:2]),
                   inc=cinc(s_act))
            rstd_v = s_act.n
            rec.op("vector",
                   lambda E: E.scalar_tensor_tensor(r1row[:, 0:W], bsum[0:1, 0:W],
                                                    1.0 / D, r2row[:, 0:W],
                                                    ALU.mult, ALU.mult),
                   waits=[(s_act, rstd_v)], inc=cinc(s_dve))
            mr_v = s_dve.n
            t_mm(bcA[:, 0:W], on1[:], r2row[:, 0:W], True, True,
                 waits=[(s_dve, mr_v)], inc=cinc(s_pe))
            t_mm(bcB[:, 0:W], on1[:], r1row[:, 0:W], True, True,
                 waits=[(s_dve, mr_v)], inc=cinc(s_pe))
            bc_v = s_pe.n
            xn_vc = [0] * DC
            for c in range(DC):
                sl = c % 2
                rec.op("vector",
                       lambda E, c=c, sl=sl: E.tensor_tensor(
                           scr[sl][:, 0:W], buf[:, c * XE + off: c * XE + off + W],
                           bcA[:, 0:W], ALU.mult),
                       waits=[(s_pe, bc_v)] if c == 0 else (),
                       inc=cinc(s_dve))
                w2_ = [(s_pe, c_["xn_rd"])] if (c == 0 and c_["xn_rd"]) else []
                rec.op("vector",
                       lambda E, c=c, sl=sl: E.tensor_tensor(
                           xn[:, c * MAXE: c * MAXE + W], scr[sl][:, 0:W],
                           bcB[:, 0:W], ALU.subtract),
                       waits=w2_, inc=cinc(s_dve))
                xn_vc[c] = s_dve.n
            xn_v = s_dve.n
            bank_free[c_["sbk"][0]] = [(s_dve, xn_v)]
            bank_free[c_["sbk"][1]] = [(s_dve, xn_v)]
            if False:
                yield
            return xn_vc

        w1f = {"v": {0: 0, 1: 0}, "freed": {0: 0, 1: 0}}
        w2f = {"v": {0: 0, 1: 0}, "freed": {0: 0, 1: 0}}
        qkf = {"v": {0: 0, 1: 0}, "freed": {0: 0, 1: 0}}
        last_ffn_resid = [0]

        ffn2_h_emitted = {}

        def ffn_emit(fbh, fbo, xn_c, scr, buf, off, W, first_tile, xn_vc,
                     mid_hook=None, h_flag=None):
            xn = xn_c["xn"]
            evac_last = 0
            for hc in range(HC):
                gidx, slot = hc // 4, (hc // 4) % 2
                if first_tile and hc % 8 == 0:
                    piece = hc // 8
                    for kc in range(DC):
                        def _wr(sl, piece=piece, kc=kc):
                            return lambda E, sl=sl: E.dma_start(
                                out=scw1_g[piece * 2: piece * 2 + 2, :, kc, :].rearrange(
                                    "g p c -> p g c"),
                                in_=pbs(sl).rearrange(
                                    "p (g c) -> p g c", c=512))
                        conv_step(piece * DC + kc,
                                  w1t[kc * 128:(kc + 1) * 128,
                                      piece * 1024:(piece + 1) * 1024],
                                  sign_and_write(_wr))
                    snap = [(s, s.n) for s in s_scwr]
                    w1grp_wait[piece * 2] = snap
                    w1grp_wait[piece * 2 + 1] = snap
                    for kc2 in range(piece * 8, piece * 8 + 8):
                        def _wr2(sl, kc2=kc2):
                            return lambda E, sl=sl: E.dma_start(
                                out=scw2_m[:, :, kc2, :].rearrange("m p c -> p m c"),
                                in_=pbs(sl).rearrange(
                                    "p (m c) -> p m c", c=128))
                        conv_step(32 + kc2, w2t[kc2 * 128:(kc2 + 1) * 128, :],
                                  sign_and_write(_wr2))
                    if piece == 3:
                        deferred['w2_scr'] = [(s, s.n) for s in s_scwr]
                if hc % 4 == 0:
                    w = list(w1grp_wait[gidx]) if first_tile else []
                    if w1f["freed"][slot]:
                        w.append((s_pe, w1f["freed"][slot]))
                    w1f["v"][slot] = sdma(
                        s_fw1[slot],
                        w1s[:, slot * 4096:(slot + 1) * 4096].rearrange(
                            "p (k c) -> p k c", c=512),
                        scw1_g[gidx], waits=w)
                bid = fbh[hc % 2]
                for kc in range(DC):
                    waits = ([(s_fw1[slot], w1f["v"][slot])] + bank_wait(bid)) if kc == 0 else []
                    if hc == 0:
                        waits = waits + [(s_dve, xn_vc[kc])]
                    t_mm(PB[bid][:, 0:W],
                         w1s[:, slot * 4096 + kc * 512 + (hc % 4) * 128:
                             slot * 4096 + kc * 512 + (hc % 4) * 128 + 128],
                         xn[:, kc * MAXE: kc * MAXE + W],
                         start=(kc == 0), stop=(kc == DC - 1),
                         waits=waits, inc=cinc(s_pe) if kc == DC - 1 else None)
                mm_v = s_pe.n
                if hc % 4 == 3:
                    w1f["freed"][slot] = mm_v
                wg = [(s_dve, a1col_v[hc]), (s_pe, mm_v)]
                rec.op("scalar",
                       lambda E, hc=hc, bid=bid: E.activation(
                           g[:, hc * MAXE: hc * MAXE + W], PB[bid][:, 0:W], AF.Gelu,
                           scale=alpha1[:, hc:hc + 1], bias=b1s[:, hc:hc + 1]),
                       waits=wg,
                       inc=cinc(s_act))
                evac_last = s_act.n
                bank_free[bid] = [(s_act, evac_last)]
                yield
            xn_c["xn_rd"] = s_pe.n
            if h_flag is not None:
                ffn2_h_emitted[h_flag] = True
            if mid_hook is not None:
                mid_hook()
            resid_free = {0: 0, 1: 0}
            for mc in range(DC):
                slot = mc % 2
                w = list(deferred['w2_scr']) if first_tile else []
                if w2f["freed"][slot]:
                    w.append((s_pe, w2f["freed"][slot]))
                w2f["v"][slot] = sdma(
                    s_fw2[slot],
                    w2s[:, slot * 4096:(slot + 1) * 4096].rearrange(
                        "p (k c) -> p k c", c=128),
                    scw2_m[mc], waits=w)
                bid = fbo[mc % 2]
                bw = bank_wait(bid)
                for kc2 in range(HC):
                    waits = ([(s_fw2[slot], w2f["v"][slot]), (s_act, evac_last)] + bw) \
                        if kc2 == 0 else []
                    t_mm(PB[bid][:, 0:W],
                         w2s[:, slot * 4096 + kc2 * 128: slot * 4096 + kc2 * 128 + 128],
                         g[:, kc2 * MAXE: kc2 * MAXE + W],
                         start=(kc2 == 0), stop=(kc2 == HC - 1),
                         waits=waits, inc=cinc(s_pe) if kc2 == HC - 1 else None)
                mm_v = s_pe.n
                w2f["freed"][slot] = mm_v
                sl = mc % 2
                w = [(s_dve, a2col_v[mc]), (s_pe, mm_v)]
                if resid_free[sl]:
                    w.append((s_dve, resid_free[sl]))
                rec.op("scalar",
                       lambda E, bid=bid, sl=sl, mc=mc: E.activation(
                           scr[sl][:, 0:W], PB[bid][:, 0:W], AF.Identity,
                           scale=alpha2[:, mc:mc + 1], bias=b2s[:, mc:mc + 1]),
                       waits=w, inc=cinc(s_act))
                ev = s_act.n
                rec.op("vector",
                       lambda E, mc=mc, sl=sl: E.tensor_add(
                           buf[:, mc * XE + off: mc * XE + off + W],
                           buf[:, mc * XE + off: mc * XE + off + W],
                           scr[sl][:, 0:W]),
                       waits=[(s_act, ev)], inc=cinc(s_dve))
                resid_free[sl] = s_dve.n
                bank_free[bid] = [(s_act, ev)]
                yield
            last_ffn_resid[0] = s_dve.n

        phi_done = {0: {}, 1: {}}
        v_done = {0: {}, 1: {}}
        att_done = {}
        qread = {0: 0, 1: 0}
        attnT_rd = [0]

        def qkv_emit(E, first_tile, xn_vc, parity, init_ones=False):
            qphi, kphi, vtok = qphi2[parity], kphi2[parity], vtok2[parity]
            xn, scr = S1c["xn"], S1c["scr"]
            scrw = list(deferred['qkv_scr']) if first_tile else []
            if first_tile:
                scrw.append((s_gpe, deferred['wout_done']))
            qkrot_free = {0: 0, 1: 0}
            phi_free = {0: 0, 1: 0}
            scr_free = {0: 0, 1: 0}
            qk_ev = {}

            def qk_part(oc):
                slot = oc % 2
                w = list(scrw)
                if qkf["freed"][slot]:
                    w.append((s_pe, qkf["freed"][slot]))
                qkf["v"][slot] = sdma(
                    s_fqk[slot],
                    qks[:, slot * 1024:(slot + 1) * 1024].rearrange(
                        "p (k c) -> p k c", c=128),
                    scq_o[oc], waits=w)
                bid = 2 + slot
                bw = bank_wait(bid)
                for kc in range(DC):
                    waits = ([(s_fqk[slot], qkf["v"][slot])] + bw) if kc == 0 else []
                    if oc == 0:
                        waits = waits + [(s_dve, xn_vc[kc])]
                    t_mm(PB[bid][:, 0:E],
                         qks[:, slot * 1024 + kc * 128: slot * 1024 + kc * 128 + 128],
                         xn[:, kc * MAXE: kc * MAXE + E],
                         start=(kc == 0), stop=(kc == DC - 1),
                         waits=waits, inc=cinc(s_pe) if kc == DC - 1 else None)
                mm_v = s_pe.n
                qkf["freed"][slot] = mm_v
                w = [(s_pe, mm_v)]
                if qkrot_free[slot]:
                    w.append((s_pe, qkrot_free[slot]))
                rec.op("scalar",
                       lambda E_, sl=slot, bid=bid: E_.copy(
                           qkrot[:, sl * MAXE: sl * MAXE + E], PB[bid][:, 0:E]),
                       waits=w, inc=cinc(s_act))
                qk_ev[oc] = s_act.n
                bank_free[bid] = [(s_act, qk_ev[oc])]

            def phi_part(oc):
                slot = oc % 2
                pbid = 0 + slot
                head = oc % 8
                w = [(s_act, qk_ev[oc])] + bank_wait(pbid)
                if phi_free[slot]:
                    w.append((s_dve, phi_free[slot]))
                w.append((s_gpe, projb_v))
                t_mm(PB[pbid][0:FEAT, 0:E], projb[:],
                     qkrot[:, slot * MAXE: slot * MAXE + E],
                     True, True, waits=w, inc=cinc(s_pe))
                phi_v = s_pe.n
                qkrot_free[slot] = phi_v
                rows = (head % 2) * 64
                w = [(s_pe, phi_v)]
                if scr_free[slot]:
                    w.append((s_dve, scr_free[slot]))
                rec.op("scalar",
                       lambda E_, pbid=pbid, rows=rows: E_.activation(
                           scr[0][rows:rows + 64, 0:E], PB[pbid][0:FEAT, 0:E], AF.Relu),
                       waits=w, inc=cinc(s_act))
                r_v = s_act.n
                rec.op("vector",
                       lambda E_, pbid=pbid, rows=rows: E_.tensor_sub(
                           scr[1][rows:rows + 64, 0:E], PB[pbid][0:FEAT, 0:E],
                           scr[0][rows:rows + 64, 0:E]),
                       waits=[(s_act, r_v)], inc=cinc(s_dve))
                m_v = s_dve.n
                phi_free[slot] = m_v
                bank_free[pbid] = [(s_dve, m_v)]
                rec.op("scalar",
                       lambda E_, rows=rows: E_.activation(
                           scr[2][rows:rows + 64, 0:E], scr[1][rows:rows + 64, 0:E],
                           AF.Exp),
                       waits=[(s_dve, m_v)], inc=cinc(s_act))
                e_v = s_act.n
                dst = qphi if oc < 8 else kphi
                pair = head // 2
                w = [(s_act, e_v)]
                if oc in (0, 8) and qread[parity]:
                    w.append((s_pe, qread[parity]))
                rec.op("vector",
                       lambda E_, dst=dst, pair=pair, rows=rows: E_.tensor_add(
                           dst[rows:rows + 64, pair * MAXE: pair * MAXE + E],
                           scr[2][rows:rows + 64, 0:E], scr[0][rows:rows + 64, 0:E]),
                       waits=w, inc=cinc(s_dve))
                scr_free[slot] = s_dve.n
                phi_done[parity][oc] = s_dve.n

            for step in range(17):
                if step < 16:
                    qk_part(step)
                if step >= 1:
                    phi_part(step - 1)
                yield
            S1c["xn_rd"] = s_pe.n
            if init_ones:
                for h in range(HEADS):
                    for tc in range(4):
                        col = (h * 4 + tc) * VST + 128
                        w = [(s_dve, deferred['alpha_done'])] if (h == 0 and tc == 0) else []
                        rec.op("gpsimd",
                               lambda E_, c=col: E_.memset(vtok[:, c:c + 1], 1.0),
                               waits=w, inc=cinc(s_gpe))
                v_ones[parity] = s_gpe.n
            nch = (E + 127) // 128
            vr_free = {0: 0, 1: 0}
            first_copy = True
            for h2 in range(2):
                vmm_v = 0
                for kc in range(DC):
                    slot = kc % 2
                    w = list(scrw)
                    if vr_free[slot]:
                        w.append((s_pe, vr_free[slot]))
                    v_l = sdma(s_fvr[slot],
                               vrs[:, slot * 512:(slot + 1) * 512].rearrange(
                                   "p (o c) -> p o c", c=128),
                               scq_o[16 + h2 * 4: 16 + h2 * 4 + 4, :, kc, :].rearrange(
                                   "o p c -> p o c"),
                               waits=w)
                    for tc in range(nch):
                        rows = min(128, E - tc * 128)
                        bid = tc
                        waits = [(s_fvr[slot], v_l)] + (bank_wait(bid) if kc == 0 else [])
                        t_mm(PB[bid][0:rows, 0:512],
                             xn[:, kc * MAXE + tc * 128: kc * MAXE + tc * 128 + rows],
                             vrs[:, slot * 512:(slot + 1) * 512],
                             start=(kc == 0), stop=(kc == DC - 1),
                             waits=waits,
                             inc=cinc(s_pe) if tc == nch - 1 else None)
                    vr_free[slot] = s_pe.n
                    yield
                vmm_v = s_pe.n
                for tc in range(nch):
                    rows = min(128, E - tc * 128)
                    bid = tc
                    for hh in range(4):
                        head = h2 * 4 + hh
                        col = (head * 4 + tc) * VST
                        w = [(s_pe, vmm_v)]
                        if first_copy:
                            first_copy = False
                            if qread[parity]:
                                w.append((s_pe, qread[parity]))
                            if init_ones:
                                w.append((s_gpe, v_ones[parity]))
                        rec.op("scalar",
                               lambda E_, bid=bid, rows=rows, hh=hh, col=col, vtok=vtok:
                               E_.copy(vtok[0:rows, col:col + 128],
                                       PB[bid][0:rows, hh * 128: hh * 128 + 128]),
                               waits=w, inc=cinc(s_act))
                    bank_free[bid] = [(s_act, s_act.n)]
                v_done[parity][h2] = s_act.n
            S1c["xn_rd"] = s_pe.n

        attn_state = {"smA_rd": {0: 0, 1: 0}, "an_rd": {0: 0, 1: 0},
                      "bankA": {0: [], 1: []}, "bankO": {0: [], 1: []},
                      "pt": {0: 0, 1: 0}, "an_hist": {0: 0, 1: 0}}

        def attn_emit(T, E, first, last, parity):
            qphi, kphi, vtok = qphi2[parity], kphi2[parity], vtok2[parity]
            nqb = (T + 127) // 128
            N = nqb * HEADS
            smA_rd, an_rd = attn_state["smA_rd"], attn_state["an_rd"]
            bankA, bankO = attn_state["bankA"], attn_state["bankO"]
            pt_free = attn_state["pt"]
            st = [dict() for _ in range(N)]
            first_cp = [True]

            def geom(c):
                m, head = c // HEADS, c % HEADS
                qb = min(128, T - m * 128)
                return m, head, qb, min(128, E - m * 128), max(0, qb - 122)

            def stage(s, c):
                m, head, qb, arows, brows = geom(c)
                slot = c % 2
                bid, bidT = 4 + slot, 6 + slot
                pair, rows = head // 2, (head % 2) * 64
                S = st[c]
                if s == 0:
                    w = list(bankA[slot]) + bank_wait(bid)
                    if m == 0:
                        w += [(s_dve, phi_done[parity][head]),
                              (s_dve, phi_done[parity][8 + head])]
                    t_mm(PB[bid][0:arows, 0:qb],
                         kphi[rows:rows + 64,
                              pair * MAXE + m * 128: pair * MAXE + m * 128 + arows],
                         qphi[rows:rows + 64,
                              pair * MAXE + 3 + m * 128: pair * MAXE + 3 + m * 128 + qb],
                         True, True, waits=w,
                         inc=cinc(s_pe) if brows == 0 else None)
                    if brows > 0:
                        t_mm(PB[bid][0:brows, 128:128 + qb],
                             kphi[rows:rows + 64,
                                  pair * MAXE + (m + 1) * 128:
                                  pair * MAXE + (m + 1) * 128 + brows],
                             qphi[rows:rows + 64,
                                  pair * MAXE + 3 + m * 128:
                                  pair * MAXE + 3 + m * 128 + qb],
                             True, True, waits=(), inc=cinc(s_pe))
                    S["sc"] = s_pe.n
                elif s == 1:
                    mvar = 0 if (first and m == 0) else (2 if (last and m == nqb - 1) else 1)
                    w = [(s_pe, S["sc"])]
                    if smA_rd[slot]:
                        w.append((s_pe, smA_rd[slot]))
                    rec.op("vector",
                           lambda E_, bid=bid, slot=slot, arows=arows, qb=qb, mvar=mvar:
                           E_.tensor_tensor(
                               smA[0:arows, slot * 128: slot * 128 + qb],
                               PB[bid][0:arows, 0:qb],
                               mA[0:arows, mvar * 128: mvar * 128 + qb], ALU.mult),
                           waits=w,
                           inc=cinc(s_dve) if brows == 0 else None)
                    if brows > 0:
                        rec.op("vector",
                               lambda E_, bid=bid, slot=slot, brows=brows, qb=qb:
                               E_.tensor_tensor(
                                   smB[0:brows, slot * 128: slot * 128 + qb],
                                   PB[bid][0:brows, 128:128 + qb],
                                   mB[0:brows, 0:qb], ALU.mult),
                               inc=cinc(s_dve))
                    S["sm"] = s_dve.n
                    bankA[slot] = [(s_dve, S["sm"])]
                elif s == 2:
                    w = [(s_dve, S["sm"]), (s_gpe, v_ones[parity]),
                         (s_act, v_done[parity][head // 4])] + list(bankO[slot])
                    t_mm(PB[bid][0:qb, 256:385],
                         smA[0:arows, slot * 128: slot * 128 + qb],
                         vtok[0:arows, (head * 4 + m) * VST: (head * 4 + m) * VST + 129],
                         True, (brows == 0), waits=w,
                         inc=cinc(s_pe) if brows == 0 else None)
                    if brows > 0:
                        t_mm(PB[bid][0:qb, 256:385],
                             smB[0:brows, slot * 128: slot * 128 + qb],
                             vtok[0:brows,
                                  (head * 4 + m + 1) * VST: (head * 4 + m + 1) * VST + 129],
                             False, True, waits=(), inc=cinc(s_pe))
                    S["at"] = s_pe.n
                    smA_rd[slot] = S["at"]
                elif s == 3:
                    w = [(s_pe, S["at"])]
                    an_prev = st[c - 2]["an"] if c >= 2 else attn_state["an_hist"][slot]
                    if an_prev:
                        w.append((s_act, an_prev))
                    rec.op("vector",
                           lambda E_, bid=bid, qb=qb, slot=slot: E_.tensor_scalar(
                               zcol[0:qb, 2 * slot: 2 * slot + 1],
                               PB[bid][0:qb, 384:385], EPS_Z, None, ALU.add),
                           waits=w)
                    rec.op("vector",
                           lambda E_, qb=qb, slot=slot: E_.reciprocal(
                               zcol[0:qb, 2 * slot + 1: 2 * slot + 2],
                               zcol[0:qb, 2 * slot: 2 * slot + 1]),
                           inc=cinc(s_dve))
                    S["zr"] = s_dve.n
                elif s == 4:
                    w = [(s_dve, S["zr"])]
                    if an_rd[slot]:
                        w.append((s_pe, an_rd[slot]))
                    rec.op("scalar",
                           lambda E_, bid=bid, qb=qb, slot=slot: E_.activation(
                               attn_n[0:qb, slot * 128: slot * 128 + 128],
                               PB[bid][0:qb, 256:384], AF.Identity,
                               scale=zcol[0:qb, 2 * slot + 1: 2 * slot + 2]),
                           waits=w, inc=cinc(s_act))
                    S["an"] = s_act.n
                    bankO[slot] = [(s_act, S["an"])]
                elif s == 5:
                    w = [(s_act, S["an"])]
                    if pt_free[slot]:
                        w.append((s_act, pt_free[slot]))
                    rec.op("tensor",
                           lambda E_, bidT=bidT, slot=slot, qb=qb: E_.transpose(
                               PB[bidT][0:128, 0:qb],
                               attn_n[0:qb, slot * 128: slot * 128 + 128],
                               id32[0:qb, 0:qb]),
                           waits=w, inc=cinc(s_pe))
                    S["tr"] = s_pe.n
                    an_rd[slot] = S["tr"]
                else:
                    w = [(s_pe, S["tr"])]
                    if first_cp[0]:
                        first_cp[0] = False
                        if attnT_rd[0]:
                            w.append((s_pe, attnT_rd[0]))
                        else:
                            w += list(deferred['qkv_scr'])
                    rec.op("scalar",
                           lambda E_, bidT=bidT, head=head, m=m, qb=qb: E_.copy(
                               attnT[:, head * MAXE + m * 128: head * MAXE + m * 128 + qb],
                               PB[bidT][0:128, 0:qb]),
                           waits=w, inc=cinc(s_act))
                    pt_free[slot] = s_act.n
                    if m == nqb - 1:
                        att_done[head] = s_act.n

            GROUPS = [(0, 1), (2, 3, 4), (5, 6)]
            for t in range(N + len(GROUPS) - 1):
                for gi in range(len(GROUPS) - 1, -1, -1):
                    c = t - gi
                    if 0 <= c < N:
                        for s in GROUPS[gi]:
                            stage(s, c)
                yield
            qread[parity] = s_pe.n
            bank_free[4] = bankA[0] + bankO[0]
            bank_free[5] = bankA[1] + bankO[1]
            bank_free[6] = [(s_act, pt_free[0])]
            bank_free[7] = [(s_act, pt_free[1])]
            attn_state["an_hist"][0] = st[N - 2]["an"]
            attn_state["an_hist"][1] = st[N - 1]["an"]

        def outproj_emit(buf, T):
            scr = S2c["scr"]
            resid_free = {0: 0, 1: 0}
            for mc in range(DC):
                bid = 4 + (mc % 2)
                bw = bank_wait(bid)
                for kc in range(DC):
                    waits = ([(s_gpe, deferred['wout_done'])] + bw) if kc == 0 else []
                    if mc == 0:
                        waits = waits + [(s_act, att_done[kc])]
                    t_mm(PB[bid][:, 0:T],
                         woutb[:, kc * 1024 + mc * 128: kc * 1024 + mc * 128 + 128],
                         attnT[:, kc * MAXE: kc * MAXE + T],
                         start=(kc == 0), stop=(kc == DC - 1),
                         waits=waits, inc=cinc(s_pe) if kc == DC - 1 else None)
                mm_v = s_pe.n
                sl = mc % 2
                w = [(s_pe, mm_v)]
                if resid_free[sl]:
                    w.append((s_dve, resid_free[sl]))
                rec.op("scalar",
                       lambda E_, bid=bid, sl=sl, mc=mc, T=T: E_.activation(
                           scr[sl][:, 0:T], PB[bid][:, 0:T], AF.Identity,
                           bias=bouts[:, mc:mc + 1]),
                       waits=w, inc=cinc(s_act))
                ev = s_act.n
                rec.op("vector",
                       lambda E_, mc=mc, sl=sl, T=T: E_.tensor_add(
                           buf[:, mc * XE + 3: mc * XE + 3 + T],
                           buf[:, mc * XE + 3: mc * XE + 3 + T], scr[sl][:, 0:T]),
                       waits=[(s_act, ev)], inc=cinc(s_dve))
                resid_free[sl] = s_dve.n
                bank_free[bid] = [(s_act, ev)]
                yield
            attnT_rd[0] = s_pe.n

        def emit_store(t, resid_v):
            for c in range(DC):
                gdma(s_out, yT[c * 128:(c + 1) * 128, offs[t]: offs[t] + TILES[t]],
                     xb[t % 2][:, c * XE + 3: c * XE + 3 + TILES[t]],
                     waits=[(s_dve, resid_v)] if c == 0 else ())
            out_v[t] = s_out.n

        xnb_vc = {}
        ffn2_resid = {}

        def s1_window(j):
            if j >= 1:
                t = j - 1
                yield from ffn_emit((2, 3), (2, 3), S2c, scrA, xb[t % 2], 3,
                                    TILES[t], False, xnb_vc[t], h_flag=t)
                ffn2_resid[t] = last_ffn_resid[0]
                emit_store(t, ffn2_resid[t])
            if j + 1 < NT:
                t2 = j + 1
                if t2 >= 2:
                    load_x(t2)
                yield
                E2 = TILES[t2] + 6
                xv = yield from ln_emit(S1c, xb[t2 % 2], 0, E2,
                                        buf_ready=xload_v[t2])
                yield from ffn_emit((2, 3), (2, 3), S1c, scrA, xb[t2 % 2], 0,
                                    E2, False, xv)
                xv = yield from ln_emit(S1c, xb[t2 % 2], 0, E2)
                yield from qkv_emit(E2, False, xv, parity=t2 % 2,
                                    init_ones=(t2 == 1))

        def s2_window(j):
            E = TILES[j] + 6
            yield from attn_emit(TILES[j], E, first=(j == 0), last=(j == NT - 1),
                                 parity=j % 2)
            yield from outproj_emit(xb[j % 2], TILES[j])
            while j >= 1 and not ffn2_h_emitted.get(j - 1):
                yield
            xnb_vc[j] = yield from ln_emit(S2c, xb[j % 2], 3, TILES[j])
            yield

        def drain(gen):
            while True:
                try:
                    next(gen)
                except StopIteration as e:
                    return e.value

        def interleave(gb, ga, head=6, ga_per_gb=3):
            for _ in range(head):
                if next(gb, StopIteration) is StopIteration:
                    gb = None
                    break
            alive = [x for x in (gb, ga) if x is not None]
            while alive:
                if gb in alive:
                    if next(gb, StopIteration) is StopIteration:
                        alive.remove(gb)
                if ga in alive:
                    for _ in range(ga_per_gb):
                        if next(ga, StopIteration) is StopIteration:
                            alive.remove(ga)
                            break

        for e in ("tensor", "scalar", "vector"):
            rec.op(e, None, waits=[xload_v[0], (s_const, consts_v)])
        E0 = TILES[0] + 6
        xv0 = drain(ln_emit(S1c, xb[0], 0, E0))
        emit_alpha1()
        drain(ffn_emit((2, 3), (2, 3), S1c, scrA, xb[0], 0, E0, True, xv0,
                       mid_hook=emit_deferred_rest))
        xv0 = drain(ln_emit(S1c, xb[0], 0, E0))
        drain(qkv_emit(E0, True, xv0, parity=0))

        for j in range(NT):
            interleave(s2_window(j), s1_window(j))

        t = NT - 1
        drain(ffn_emit((2, 3), (2, 3), S2c, scrA, xb[t % 2], 3,
                       TILES[t], False, xnb_vc[t]))
        emit_store(t, last_ffn_resid[0])

        rec.op("gpsimd", None, waits=[(s_out, out_v[NT - 1])])

        block = ctx.enter_context(nc.Block())

        def emit_for(name):
            extra = {"gpsimd": {"gpsimdq"}, "scalar": {"scalarq"}}
            names = {name} | extra.get(name, set())

            def emitter(engobj):
                for eng, fn, waits, incs in rec.ops:
                    if eng not in names:
                        continue
                    for s, v in waits:
                        engobj.wait_ge(s.h, v)
                    if fn is None:
                        if incs:
                            inst = engobj.nop()
                            for s, amt in incs:
                                inst.then_inc(s.h, amt)
                        continue
                    inst = fn(engobj)
                    for s, amt in incs:
                        inst.then_inc(s.h, amt)
            return emitter

        block.tensor(emit_for("tensor"))
        block.scalar(emit_for("scalar"))
        block.vector(emit_for("vector"))
        block.sync(emit_for("sync"))
        block.gpsimd(emit_for("gpsimd"))

    return nc


def _host_inputs(L0_attn_x, L0_attn_w_qkv, L0_attn_proj, L0_attn_w_out, L0_attn_b_out,
                 L0_ffn_w1, L0_ffn_b1, L0_ffn_w2, L0_ffn_b2):
    x = np.asarray(L0_attn_x, dtype=np.float32)
    wqkv = np.ascontiguousarray(np.asarray(L0_attn_w_qkv, np.float32))
    projm = np.ascontiguousarray(np.asarray(L0_attn_proj, np.float32))
    woutm = np.ascontiguousarray(np.asarray(L0_attn_w_out, np.float32))
    bout = np.asarray(L0_attn_b_out, np.float32)
    w1 = np.asarray(L0_ffn_w1, np.float32)
    b1 = np.asarray(L0_ffn_b1, np.float32)
    w2 = np.asarray(L0_ffn_w2, np.float32)
    b2 = np.asarray(L0_ffn_b2, np.float32)

    w1t = np.ascontiguousarray(w1.T)
    w2t = np.ascontiguousarray(w2.T)
    b1cm = np.ascontiguousarray(b1.reshape(HC, 128).T)
    b2cm = np.ascontiguousarray(b2.reshape(DC, 128).T)
    boutcm = np.ascontiguousarray(bout.reshape(DC, 128).T)
    identf = np.eye(128, dtype=np.float32)
    onesb = np.ones((128, 1), dtype=ml_dtypes.bfloat16)
    ones1 = np.ones((1, 128), dtype=np.float32)
    cconst = np.broadcast_to(np.array([EPS_LN, -0.5, 1.0 / D], np.float32),
                             (128, 3)).copy()

    j = np.arange(128)[:, None]
    q = np.arange(128)[None, :]
    bandA = ((j - q >= 0) & (j - q <= 6)).astype(np.float32)
    jb = np.arange(6)[:, None]
    bandB = (q >= 122 + jb).astype(np.float32)
    T_last = TILES[-1]
    cut_last = (T_last + 3) - 384

    in_maps = []
    for core in range(8):
        b = core // 4
        start = (core % 4) * NTOK
        lo, hi = start - 3, start + NTOK + 3
        xs = np.zeros((NTOK + 6, D), np.float32)
        clo, chi = max(lo, 0), min(hi, 16384)
        xs[clo - lo: chi - lo, :] = x[b, clo:chi, :]
        xTm = np.ascontiguousarray(xs.T)

        Afirst = bandA.copy()
        Alast = bandA.copy()
        if start == 0:
            Afirst[0:3, :] = 0.0
        if start + NTOK == 16384:
            Alast[cut_last:cut_last + 3, :] = 0.0
        maskA3m = np.ascontiguousarray(np.concatenate([Afirst, bandA, Alast], axis=1))

        in_maps.append({
            "xT": xTm, "wqkv": wqkv, "wout": woutm, "proj": projm,
            "w1t": w1t, "w1n": w1, "w2t": w2t, "w2n": w2,
            "b1c": b1cm, "b2c": b2cm, "boutc": boutcm,
            "maskA3": maskA3m, "maskB": np.ascontiguousarray(bandB),
            "identf": identf, "onesb": onesb, "ones1": ones1, "cconst": cconst,
        })
    return in_maps


def kernel(L0_attn_x, L0_attn_w_qkv, L0_attn_proj, L0_attn_w_out, L0_attn_b_out,
           L0_ffn_w1, L0_ffn_b1, L0_ffn_w2, L0_ffn_b2):
    in_maps = _host_inputs(L0_attn_x, L0_attn_w_qkv, L0_attn_proj, L0_attn_w_out,
                           L0_attn_b_out, L0_ffn_w1, L0_ffn_b1, L0_ffn_w2, L0_ffn_b2)
    if "nc" not in _CACHED:
        _CACHED["nc"] = build_nc()
    nc = _CACHED["nc"]
    res = run_bass_kernel_spmd(nc, in_maps, core_ids=list(range(8)))
    out = np.empty((2, 16384, D), np.float32)
    for core in range(8):
        b = core // 4
        start = (core % 4) * NTOK
        out[b, start:start + NTOK, :] = res.results[core]["yT"].T
    return out
